# version 21 (final)
# speedup vs baseline: 1.1313x; 1.0112x over previous
import os
import numpy as np
import concourse.bass as bass
import concourse.mybir as mybir
from concourse.bass_utils import run_bass_kernel_spmd

F32 = mybir.dt.float32
FR = mybir.dt.float32r
AF = mybir.ActivationFunctionType
ALU = mybir.AluOpType
AX = mybir.AxisListType

MM_EXACT = os.environ.get("MM_EXACT", "0") == "1"
MMDT = F32 if MM_EXACT else FR

B, S, D = 2, 8192, 1024
R, NT, STW = 2048, 16, 2
NST = NT // STW
E = 16
NEG = -1e9
EPS = 1e-5
RING_N = 9
RING_X = 3
TAP_NAMES = [t for t in os.environ.get("TAPS", "").split(",") if t]
KPH = int(os.environ.get("KPHASES", "3"))
KATT = int(os.environ.get("KATT", "99"))
KSUB = int(os.environ.get("KSUB", "4"))
KLAT = int(os.environ.get("KLAT", "9"))


class Sem:
    __slots__ = ("name", "step", "n", "h")

    def __init__(self, name, step=1):
        self.name, self.step, self.n, self.h = name, step, 0, None


class Prog:
    def __init__(self):
        self.events = {e: [] for e in ("tensor", "vector", "scalar", "gpsimd", "sync")}
        self.sems = []
        self.self_sem = {e: self.sem("self_" + e) for e in ("vector", "scalar", "gpsimd")}

    def sem(self, name, step=1):
        s = Sem(name, step)
        self.sems.append(s)
        return s

    def emit(self, eng, build, waits=(), inc=(), selfser=True):
        waits = [w for w in waits if w is not None]
        incs = [inc] if isinstance(inc, Sem) else list(inc)
        if eng in self.self_sem and selfser:
            ss = self.self_sem[eng]
            if ss.n:
                waits.append((ss, ss.n))
            incs.append(ss)
        toks = []
        for s in incs:
            s.n += s.step
            toks.append((s, s.n))
        self.events[eng].append((waits, build, incs))
        return toks

    def bar(self, eng):
        ss = self.self_sem[eng]
        return (ss, ss.n) if ss.n else None


def build_program():
    nc = bass.Bass("TRN2", target_bir_lowering=False, detect_race_conditions=False)
    P = Prog()
    taps = []

    def din(name, shape, dt=MMDT):
        return nc.dram_tensor(name, list(shape), dt, kind="ExternalInput").ap()

    d = {}
    d["xfull"] = din("xfull", [S, D], F32)
    d["xown"] = din("xown", [R, D], F32)
    d["xhalo"] = din("xhalo", [128, D], F32)
    for nm, sh in [("qkv_w", [D, 3 * D]), ("rq_w", [D, 256]), ("rkv_w", [D, 512]),
                   ("rout_w", [256, D]), ("out_w", [D, D]), ("gate_w", [16, D]),
                   ("We1", [E, D, 512]), ("We2", [E, 512, D]), ("We3", [E, D, 512]),
                   ("be2", [16, D]),
                   ("ws1", [D, D]), ("ws2", [D, D]), ("ws3", [D, D]),
                   ("identr", [128, 128]), ("maskA", [128, 512]), ("maskA0", [128, 256])]:
        d[nm] = din(nm, sh)
    for nm, sh in [("qkbT", [128, 16]), ("rqbT", [128, 2]), ("rkvbT", [128, 4]),
                   ("be1T", [128, 64]), ("be3T", [128, 64]), ("bs1T", [128, 8]),
                   ("bs3T", [128, 8]), ("gbB", [128, 16]),
                   ("glwB", [128, D]), ("grwB", [128, D]), ("latmaskB", [128, NT * 64]),
                   ("oneover", [128, 1]), ("epsc", [128, 1]), ("onespage", [128, 128]),
                   ("biaspage", [128, 2048]), ("outbB", [128, 1024]), ("routbB", [128, 1024]), ("rowmask0", [128, 1]),
                   ("row0add", [128, 1]), ("latrow0", [128, 1])]:
        d[nm] = din(nm, sh, F32)
    yout = nc.dram_tensor("yout", [R, D], F32, kind="ExternalOutput").ap()

    A_COLS = 15040
    RO_A = RING_N * 512
    RO_rkT = RO_A + A_COLS
    RO_rv = RO_rkT + 256
    RO_routw = RO_rv + 256
    RO_gwT = RO_routw + 2048
    RO_attnsb = RO_gwT + 128
    RO_attnT = RO_attnsb + 1024
    RO_rwg = RO_attnT
    R_COLS = RO_rwg + 32
    AO_lnT, AO_ln, AO_qT, AO_kT, AO_v, AO_rqT = 0, 2048, 3072, 5120, 11264, 14528
    AO_latexp, AO_latT, AO_latoutT, AO_mixedT = 0, 256, 768, 1024
    AO_ln2T, AO_h = 0, 8192
    AO_pooledT, AO_rkvT = 0, 512

    B_COLS = 8192
    BO_xring, BO_lnpool = 0, 2048
    BO_local, BO_remote, BO_tmp = 0, 1024, 2048
    BO_glwB, BO_grwB, BO_latmaskB = 4096, 5120, 6144
    BO_outb = 7168
    FO_xres = 0
    FO_B = NT * 1024
    misc = {}
    _off = [FO_B + B_COLS]

    def malloc(name, cols):
        misc[name] = (_off[0], cols)
        _off[0] += cols

    for nm, cc in [("qkbT", 16), ("rqbT", 2), ("rkvbT", 4), ("be1T", 64), ("be3T", 64),
                   ("bs1T", 8), ("bs3T", 8), ("gbB", 16), ("oneover", 1), ("epsc", 1),
                   ("rowmask0", 1), ("row0add", 1), ("latrow0", 1), ("onespage", 128),
                   ("biaspage", 2048), ("combT", 128), ("zerop", 16), ("routbg", 16),
                   ("latsc", 256), ("routbB", 1024)]:
        malloc(nm, cc)
    for p in range(2):
        for nm, cc in [("stat6", 12), ("stat2", 2), ("std", 1), ("rstd", 1), ("nmr", 1),
                       ("negmax2", 2), ("bias2", 2), ("sums", 16), ("recips", 16),
                       ("negmaxL", 4), ("biasL", 4), ("latsums", 4),
                       ("latrecip", 4), ("probs", 16), ("s2", 16), ("scr16a", 16),
                       ("scr16b", 16), ("scr4a", 4), ("scr4b", 4), ("scr4c", 4),
                       ("scr1a", 1), ("scr1b", 1), ("lg", 16), ("rg", 16), ("gate", 16)]:
            malloc(f"{nm}{p}", cc)
    for p in range(2, 4):
        for nm, cc in [("stat6", 12), ("stat2", 2), ("std", 1), ("rstd", 1),
                       ("nmr", 1)]:
            malloc(f"{nm}{p}", cc)
    F_COLS = _off[0]
    assert (R_COLS + F_COLS + 128 + 768) * 4 <= 212700, \
        f"SBUF over: {(R_COLS + F_COLS + 128 + 768) * 4}"

    Rb = nc.alloc_sbuf_tensor("Rar", [128, R_COLS], MMDT).ap()
    Fb = nc.alloc_sbuf_tensor("Far", [128, F_COLS], F32).ap()
    ident_t = nc.alloc_sbuf_tensor("identsb", [128, 128], MMDT).ap()
    maskA_t = nc.alloc_sbuf_tensor("maskAsb", [128, 512], MMDT).ap()
    maskA0_t = nc.alloc_sbuf_tensor("maskA0sb", [128, 256], MMDT).ap()
    IDENT = ident_t[:, :]

    def rv_(off, cols, rows=128, r0=0):
        return Rb[r0:r0 + rows, off:off + cols]

    def fv_(off, cols, rows=128, r0=0):
        return Fb[r0:r0 + rows, off:off + cols]

    def mv(name, p=None):
        off, cols = misc[name if p is None else f"{name}{p}"]
        return Fb[0:128, off:off + cols]

    pbk = [nc.alloc_psum_tensor(f"pb{i}", [128, 512], F32).ap() for i in range(6)]
    tbk = [nc.alloc_psum_tensor(f"tb{i}", [128, 512], MMDT).ap() for i in range(2)]

    lnT = rv_(RO_A + AO_lnT, 2048).rearrange("p (c t) -> p c t", c=8)
    ln_t = rv_(RO_A + AO_ln, 1024)
    qT = rv_(RO_A + AO_qT, 2048).rearrange("p (c t) -> p c t", c=8)
    kT = rv_(RO_A + AO_kT, 6144).rearrange("p (c t) -> p c t", c=16)
    vV65 = rv_(RO_A + AO_v, 3264).rearrange("p (s h e) -> p s h e", s=3, h=16)
    vVf = rv_(RO_A + AO_v, 3264).rearrange("p (s e) -> p s e", s=3)
    rqT = rv_(RO_A + AO_rqT, 512).rearrange("p (c t) -> p c t", c=2)
    kTz_tok = [None]
    mixedT = rv_(RO_A + AO_mixedT, 1024)
    ln2T = rv_(RO_A + AO_ln2T, 8192).rearrange("p (c t) -> p c t", c=8)
    hV = rv_(RO_A + AO_h, 4096).rearrange("p (g c t) -> p g c t", g=2, c=4)
    ln2_t = rv_(RO_attnsb, 1024)
    pooledT = rv_(RO_A + AO_pooledT, 512).rearrange("p (c l) -> p c l", c=8)
    routw = rv_(RO_routw, 2048).rearrange("p (c e) -> p c e", c=2)
    gwT = rv_(RO_gwT, 128).rearrange("p (c e) -> p c e", c=8)
    attnT_sb = [rv_(RO_attnsb, 512), rv_(RO_attnsb + 512, 512)]
    xres = fv_(FO_xres, NT * 1024).rearrange("p (t e) -> p t e", t=NT)
    local_v = fv_(FO_B + BO_local, 1024)
    remote_v = fv_(FO_B + BO_remote, 1024)
    mixed_v = rv_(RO_A + AO_ln, 1024)
    glwB = fv_(FO_B + BO_glwB, 1024)
    grwB = fv_(FO_B + BO_grwB, 1024)
    latmaskB = fv_(FO_B + BO_latmaskB, 1024)
    y_sb = fv_(FO_B, 8192).rearrange("p (t e) -> p t e", t=8)
    xring = [fv_(FO_B + BO_xring + i * 1024, 1024) for i in range(2)]
    lnpool = [fv_(FO_B + BO_lnpool, 1024), fv_(FO_B + BO_lnpool, 1024)]
    ONES = mv("onespage")
    BIASP = mv("biaspage")

    t_ev = P.sem("t_ev")
    g_ld = P.sem("g_ld", 16)
    so = P.sem("so", 16)
    rg_w = [P.sem(f"rw{i}", 16) for i in range(RING_N + RING_X)]
    xr_w = [P.sem(f"xw{i}", 16) for i in range(2)]

    bank_free, bank_open = {}, {}

    def bank_wait(b):
        key = id(b.tensor)
        assert not bank_open.get(key, False), "psum bank reopened while open"
        bank_open[key] = True
        t = bank_free.get(key)
        return [t] if t else []

    def bank_close(b, tok):
        key = id(b.tensor)
        bank_open[key] = False
        bank_free[key] = tok

    SLOT_OFF = [s * 512 for s in range(RING_N)] + \
               [RO_A + 13312 + 512 * k for k in range(RING_X)]
    rst = {"next": 0, "loads": [0] * (RING_N + RING_X),
           "free": [None] * (RING_N + RING_X),
           "lhist": [[] for _ in range(RING_N + RING_X)],
           "active": RING_N}

    def ring_load(ap_fn, cols=512):
        s = rst["next"]
        rst["next"] = (s + 1) % rst["active"]
        j = rst["loads"][s]
        rst["loads"][s] += 1
        rst["lhist"][s].append(rst.get("ph"))
        assert j == 0 or rst["free"][s] is not None, \
            f"slot {s} reloaded unreleased (ph={rst.get('ph')}, load #{j}); " + \
            f"history={rst['lhist'][s]}"
        w = [rst["free"][s]] if rst["free"][s] else []
        rst["free"][s] = None
        P.emit("sync", (lambda s=s, f=ap_fn, c=cols:
                        nc.sync.dma_start(out=rv_(SLOT_OFF[s], c), in_=f())),
               waits=w, inc=[rg_w[s]], selfser=False)
        return s, (rg_w[s], 16 * rst["loads"][s])

    def ring_release(slots, tok):
        for sl in slots:
            s = sl[0] if isinstance(sl, tuple) else sl
            assert rst["free"][s] is None, f"slot {s} double release"
            rst["lhist"][s].append("REL:" + str(rst.get("ph")))
            rst["free"][s] = tok

    def slot(s, cols=512):
        return rv_(SLOT_OFF[s], cols)

    def T(build, waits=(), tok=False):
        r = P.emit("tensor", build, waits=waits, inc=([t_ev] if tok else []))
        return r[0] if tok else None

    def V(build, waits=()):
        return P.emit("vector", build, waits=waits)[-1]

    def SC(build, waits=()):
        return P.emit("scalar", build, waits=waits)[-1]

    def G(build, waits=()):
        return P.emit("gpsimd", build, waits=waits)[-1]

    def GD(build, waits=()):
        return P.emit("gpsimd", build, waits=waits, inc=[g_ld], selfser=False)[0]

    def tap(name, ap_fn, shape, waits):
        if name not in TAP_NAMES:
            return None
        t = nc.dram_tensor("tap_" + name, list(shape), F32, kind="ExternalOutput").ap()
        taps.append(name)
        return GD(lambda t=t, f=ap_fn: nc.gpsimd.dma_start(out=t, in_=f()),
                  waits=waits)

    ln_guard = {0: None, 1: None, 2: None, 3: None}
    ln_par = [0]

    def ln_stats(src0_fn, src1_fn, waits=()):
        p = ln_par[0] = (ln_par[0] + 1) % 4
        w = list(waits) + ([ln_guard[p]] if ln_guard[p] else [])
        V(lambda p=p, f=src0_fn: nc.vector.bn_stats(mv("stat6", p)[:, 0:6], f()), waits=w)
        V(lambda p=p, f=src1_fn: nc.vector.bn_stats(mv("stat6", p)[:, 6:12], f()))
        v3 = V(lambda p=p: nc.vector.bn_aggr(mv("stat2", p), mv("stat6", p)))
        s1 = SC(lambda p=p: nc.scalar.activation(mv("std", p), mv("stat2", p)[:, 1:2],
                                                 AF.Sqrt, bias=mv("epsc")[:, 0:1],
                                                 scale=1.0), waits=[v3])
        V(lambda p=p: nc.vector.reciprocal(mv("rstd", p), mv("std", p)), waits=[s1])
        nm = V(lambda p=p: nc.vector.tensor_scalar(mv("nmr", p), mv("stat2", p)[:, 0:1],
                                                   mv("rstd", p)[:, 0:1], -1.0,
                                                   ALU.mult, ALU.mult))
        return p, nm

    def transposes8(src_fn, dst_fn, waits):
        last_tt, last_cp = None, None
        for rnd in range(2):
            b = tbk[rnd]
            w = list(waits) + bank_wait(b)
            for c in range(4):
                last_tt = T(lambda b=b, c=c, s=src_fn, rnd=rnd:
                            nc.tensor.transpose(b[:, c * 128:(c + 1) * 128],
                                                s(rnd * 4 + c), IDENT),
                            waits=(w if c == 0 else []), tok=(c == 3))
            last_cp = V(lambda b=b, rnd=rnd, df=dst_fn: nc.vector.tensor_copy(
                df(rnd), b[:, 0:512].rearrange("p (c t) -> p c t", c=4)),
                waits=[last_tt])
            bank_close(b, last_cp)
        return last_tt, last_cp

    ncl = [0]

    def cload(dst_ap, src_name):
        GD(lambda a=dst_ap, s=src_name: nc.gpsimd.dma_start(out=a, in_=d[s]))
        ncl[0] += 1

    for nm in ["qkbT", "rqbT", "rkvbT", "be1T", "be3T", "bs1T", "bs3T", "gbB",
               "oneover", "epsc", "rowmask0", "row0add", "latrow0", "onespage",
               "biaspage"]:
        cload(mv(nm), nm)
    cload(fv_(FO_B + BO_outb, 1024), "outbB")
    cload(mv("routbB"), "routbB")
    cload(glwB, "glwB")
    cload(grwB, "grwB")
    cload(latmaskB, "latmaskB")
    cload(IDENT, "identr")
    cload(maskA_t[:, :], "maskA")
    cload(maskA0_t[:, :], "maskA0")
    GD(lambda: nc.gpsimd.dma_start(
        out=rv_(RO_routw, 2048).rearrange("p (c e) -> p c e", c=2),
        in_=d["rout_w"].rearrange("(c p) e -> p c e", p=128)))
    ncl[0] += 1
    gwraw = Rb[0:16, RO_attnsb:RO_attnsb + 1024]
    cload(gwraw, "gate_w")
    GD(lambda: nc.gpsimd.dma_start(out=xres[:, :, :],
                                   in_=d["xown"].rearrange("(t p) e -> p t e", p=128)))
    ncl[0] += 1
    xhalo_sb = lnpool[0]
    GD(lambda: nc.gpsimd.dma_start(out=xhalo_sb, in_=d["xhalo"]))
    ncl[0] += 1
    consts_tok = (g_ld, 16 * ncl[0])

    V(lambda: nc.vector.memset(mv("zerop"), 0.0), waits=[consts_tok])

    zk = SC(lambda: nc.scalar.activation(rv_(RO_A + AO_kT, 6144),
                                         rv_(RO_A + AO_kT, 6144), AF.Identity,
                                         bias=0.0, scale=0.0), waits=[consts_tok])
    zr = SC(lambda: nc.scalar.activation(rv_(RO_rkT, 256), rv_(RO_rkT, 256),
                                         AF.Identity, bias=0.0, scale=0.0))
    kTz_tok[0] = zr
    SC(lambda: nc.scalar.activation(
        rv_(RO_A + AO_v, 3264).rearrange("p (x e) -> p x e", e=68)[:, :, 64:68],
        rv_(RO_A + AO_v, 3264).rearrange("p (x e) -> p x e", e=68)[:, :, 64:68],
        AF.Identity, bias=1.0, scale=0.0))

    rwg = rv_(RO_rwg, 32)
    for c2 in range(2):
        for hf in range(2):
            b0 = tbk[0].bitcast(F32)
            bw0 = bank_wait(tbk[0])
            V(lambda c2=c2, hf=hf, b0=b0: nc.vector.tensor_tensor(
                b0[:, 0:512], routw[:, c2, hf * 512:(hf + 1) * 512].bitcast(F32),
                grwB[:, hf * 512:(hf + 1) * 512], ALU.mult),
                waits=bw0 + [consts_tok])
            def _rwg(c2=c2, hf=hf, b0=b0):
                with nc.allow_low_precision(reason="f32r bits are fp32"):
                    return nc.vector.reduce_sum(
                        rwg[:, c2 * 16 + hf * 8:c2 * 16 + hf * 8 + 8],
                        b0[:, 0:512].rearrange("p (h e) -> p h e", h=8), axis=AX.X)
            rt = V(_rwg)
            bank_close(tbk[0], rt)
    for hf in range(2):
        b0 = tbk[0].bitcast(F32)
        bw0 = bank_wait(tbk[0])
        V(lambda hf=hf, b0=b0: nc.vector.tensor_tensor(
            b0[0:1, 0:512], mv("routbB")[0:1, hf * 512:(hf + 1) * 512],
            grwB[0:1, hf * 512:(hf + 1) * 512], ALU.mult), waits=bw0)
        rt = V(lambda hf=hf, b0=b0: nc.vector.reduce_sum(
            mv("routbg")[0:1, hf * 8:hf * 8 + 8],
            b0[0:1, 0:512].rearrange("p (h e) -> p h e", h=8), axis=AX.X))
        bank_close(tbk[0], rt)

    gwT_tok = None
    for rnd in range(2):
        b = tbk[rnd]
        w = [consts_tok] + bank_wait(b)
        tt = None
        for c in range(4):
            cc = rnd * 4 + c
            tt = T(lambda b=b, c=c, cc=cc: nc.tensor.transpose(
                b[:, c * 16:(c + 1) * 16], gwraw[0:16, cc * 128:(cc + 1) * 128],
                IDENT[0:16, 0:16]), waits=(w if c == 0 else []), tok=(c == 3))
        cp = V(lambda b=b, rnd=rnd: nc.vector.tensor_copy(
            rv_(RO_gwT + rnd * 64, 64), b[:, 0:64]), waits=[tt])
        bank_close(b, cp)
        gwT_tok = cp

    p, nmtok = ln_stats(lambda: xhalo_sb[:, 0:512], lambda: xhalo_sb[:, 512:1024],
                        [consts_tok])
    lnh = SC(lambda p=p: nc.scalar.activation(ln_t, xhalo_sb, AF.Identity,
                                              bias=mv("nmr", p)[:, 0:1],
                                              scale=mv("rstd", p)[:, 0:1]),
             waits=[nmtok])
    ln_guard[p] = lnh
    htt, hcp = transposes8(lambda c: rv_(RO_A + AO_ln + c * 128, 128),
                           lambda rnd: lnT[:, rnd * 4:(rnd + 1) * 4, 0:128], [lnh])
    for mg in (2, 3):
        slots = [ring_load(lambda k=k, mg=mg:
                           d["qkv_w"][k * 128:(k + 1) * 128, mg * 512:(mg + 1) * 512])
                 for k in range(8)]
        for m in range(4):
            ch = (mg - 2) * 4 + m
            bk = pbk[m % 2]
            bw = bank_wait(bk) + [hcp]
            mm = None
            for k in range(8):
                s_, stk = slots[k]
                mm = T(lambda bk=bk, s_=s_, m=m, k=k: nc.tensor.matmul(
                    bk[:, 0:128], slot(s_)[:, m * 128:(m + 1) * 128], lnT[:, k, 0:128],
                    start=(k == 0), stop=(k == 7)),
                    waits=(bw if k == 0 else []) + [stk], tok=(k == 7))
            SC(lambda bk=bk, ch=ch: nc.scalar.activation(
                kT[0:64, 2 * ch, 0:128], bk[0:64, 0:128], AF.Identity,
                bias=mv("qkbT")[0:64, 8 + ch:9 + ch], scale=1.0),
               waits=[mm, kTz_tok[0]])
            ev = SC(lambda bk=bk, ch=ch: nc.scalar.activation(
                kT[64:128, 2 * ch + 1, 0:128], bk[64:128, 0:128], AF.Identity,
                bias=mv("qkbT")[64:128, 8 + ch:9 + ch], scale=1.0))
            bank_close(bk, ev)
        ring_release(slots, mm)
    rst['ph'] = 'halo_v'
    for mg in (4, 5):
        slots = [ring_load(lambda k=k, mg=mg:
                           d["qkv_w"][k * 128:(k + 1) * 128, mg * 512:(mg + 1) * 512])
                 for k in range(8)]
        bk = pbk[2 + (mg - 4)]
        bw = bank_wait(bk) + [hcp]
        for k in range(8):
            s_, stk = slots[k]
            T(lambda bk=bk, s_=s_, k=k: nc.tensor.matmul(
                bk[:, 0:512], lnT[:, k, 0:128], slot(s_)[:, 0:512],
                start=(k == 0), stop=False),
                waits=(bw if k == 0 else []) + [stk])
        mm = T(lambda bk=bk, mg=mg: nc.tensor.matmul(
            bk[:, 0:512], ONES[0:1, 0:128], BIASP[0:1, (mg - 4) * 512:(mg - 3) * 512],
            start=False, stop=True), tok=True)
        ev = SC(lambda bk=bk, mg=mg: nc.scalar.copy(
            vV65[:, 0, (mg - 4) * 8:(mg - 3) * 8, 0:64],
            bk[:, 0:512].rearrange("p (h e) -> p h e", h=8)), waits=[mm])
        b0 = tbk[0].bitcast(F32)
        bw0 = bank_wait(tbk[0])
        vgm = V(lambda bk=bk, mg=mg, b0=b0: nc.vector.tensor_tensor(
            b0[:, 0:512], bk[:, 0:512], glwB[:, (mg - 4) * 512:(mg - 3) * 512],
            ALU.mult), waits=bw0 + [mm, ev])
        bank_close(bk, vgm)
        def _vg0(mg=mg, b0=b0):
            with nc.allow_low_precision(reason="f32r bits are fp32"):
                return nc.vector.reduce_sum(
                    vV65[:, 0, (mg - 4) * 8:(mg - 3) * 8, 65:66],
                    b0[:, 0:512].rearrange("p (h e) -> p h e", h=8), axis=AX.X)
        vgr = V(_vg0)
        bank_close(tbk[0], vgr)
        ring_release(slots, mm)

    pool_bank = pbk[5]
    pool_w = bank_wait(pool_bank)
    xr_n = [0, 0]
    xr_free = {0: None, 1: None}
    last_pool = {0: None, 1: None}
    pool_mm = None
    for cch in range(64):
        sl = cch % 2
        j = xr_n[sl]
        xr_n[sl] += 1
        w = [xr_free[sl]] if xr_free[sl] else []
        P.emit("sync", (lambda sl=sl, cch=cch: nc.sync.dma_start(
            out=xring[sl], in_=d["xfull"][cch * 128:(cch + 1) * 128, :])),
            waits=w, inc=[xr_w[sl]], selfser=False)
        xtok = (xr_w[sl], 16 * xr_n[sl])
        p, nmtok = ln_stats(lambda sl=sl: xring[sl][:, 0:512],
                            lambda sl=sl: xring[sl][:, 512:1024], [xtok])
        lp = 0
        ap = SC(lambda p=p, sl=sl, lp=lp: nc.scalar.activation(
            lnpool[lp], xring[sl], AF.Identity, bias=mv("nmr", p)[:, 0:1],
            scale=mv("rstd", p)[:, 0:1]),
            waits=[nmtok, last_pool[lp]])
        ln_guard[p] = ap
        xr_free[sl] = ap
        for fc in range(8):
            pool_mm = T(lambda fc=fc, cch=cch, lp=lp: nc.tensor.matmul(
                pool_bank[:, fc * 64 + cch:fc * 64 + cch + 1],
                lnpool[lp][:, fc * 128:(fc + 1) * 128], mv("oneover")[:, 0:1],
                start=(cch == 0 and fc == 0), stop=(cch == 63 and fc == 7)),
                waits=(([ap] + pool_w) if fc == 0 else []),
                tok=(fc == 7))
        last_pool[lp] = pool_mm
    pool_ev = SC(lambda: nc.scalar.copy(rv_(RO_A + AO_pooledT, 512),
                                        pool_bank[:, 0:512]), waits=[pool_mm])
    bank_close(pool_bank, pool_ev)
    tap("pooledT", lambda: rv_(RO_A + AO_pooledT, 512).bitcast(F32), [128, 512],
        [pool_ev])

    rst['ph'] = 'rkv'
    slots = [ring_load(lambda k=k: d["rkv_w"][k * 128:(k + 1) * 128, :])
             for k in range(8)]
    rkv_toks = []
    for m in range(4):
        bk = pbk[m % 2]
        bw = bank_wait(bk) + [pool_ev]
        mm = None
        for k in range(8):
            s_, stk = slots[k]
            mm = T(lambda bk=bk, s_=s_, m=m, k=k: nc.tensor.matmul(
                bk[:, 0:64], slot(s_)[:, m * 128:(m + 1) * 128], pooledT[:, k, :],
                start=(k == 0), stop=(k == 7)),
                waits=(bw if k == 0 else []) + [stk], tok=(k == 7))
        ev = SC(lambda bk=bk, m=m: nc.scalar.activation(
            rv_(RO_A + AO_rkvT + m * 64, 64), bk[:, 0:64], AF.Identity,
            bias=mv("rkvbT")[:, m:m + 1], scale=1.0), waits=[mm])
        bank_close(bk, ev)
        rkv_toks.append(ev)
        if m == 3:
            ring_release(slots, mm)
    for lh in range(4):
        r0 = (lh % 2) * 64
        mch = lh // 2
        rk_tok = V(lambda lh=lh, r0=r0, mch=mch: nc.vector.tensor_copy(
            Rb[r0:r0 + 64, RO_rkT + lh * 64:RO_rkT + (lh + 1) * 64],
            Rb[r0:r0 + 64, RO_A + AO_rkvT + mch * 64:RO_A + AO_rkvT + (mch + 1) * 64]),
            waits=rkv_toks[0:2] + [kTz_tok[0]] if lh == 0 else [])
    bt = tbk[0]
    bw = bank_wait(bt) + rkv_toks[2:4]
    tt = None
    for m in (2, 3):
        tt = T(lambda bt=bt, m=m: nc.tensor.transpose(
            bt[0:64, (m - 2) * 128:(m - 1) * 128], rv_(RO_A + AO_rkvT + m * 64, 64),
            IDENT), waits=(bw if m == 2 else []), tok=(m == 3))
    rvv_tok = V(lambda bt=bt: nc.vector.tensor_copy(Rb[0:64, RO_rv:RO_rv + 256],
                                                    bt[0:64, 0:256]), waits=[tt])
    bank_close(bt, rvv_tok)
    tap("rkT", lambda: rv_(RO_rkT, 128).bitcast(F32), [128, 128], [rk_tok])
    tap("rvv", lambda: Rb[0:64, RO_rv:RO_rv + 256].bitcast(F32), [64, 256], [rvv_tok])

    attn_guard = [None, None]
    if KPH < 2:
        for ti in range(NT):
            P.emit("gpsimd", (lambda ti=ti: nc.gpsimd.dma_start(
                out=yout[ti * 128:(ti + 1) * 128, :], in_=xres[:, ti, :])),
                waits=[consts_tok, rvv_tok], inc=[so], selfser=False)
        P.emit("gpsimd", lambda: nc.gpsimd.engine_nop(), waits=[(so, 16 * NT)],
               selfser=False)
        for s in P.sems:
            s.h = nc.alloc_semaphore(s.name)
        with nc.Block() as block:
            def mk0(eng):
                def run(proxy):
                    hw = {}
                    for waits, build, incs in P.events[eng]:
                        for sm, v in waits:
                            if hw.get(id(sm), 0) < v:
                                proxy.wait_ge(sm.h, v)
                                hw[id(sm)] = v
                        ins = build()
                        for sm in incs:
                            ins.then_inc(sm.h, sm.step)
                return run
            block.tensor(mk0("tensor"))
            block.vector(mk0("vector"))
            block.scalar(mk0("scalar"))
            block.gpsimd(mk0("gpsimd"))
            block.sync(mk0("sync"))
        return nc, taps

    attnT_guard = [None, None]
    mixed_guard = None
    halo_guard = None
    lat_guard = None
    lnt_guard = None
    xres_tok = [None] * NT
    lnT_box = [None]
    lnT_free = [None]

    ln_pages = {}

    def emit_LNstats(st2):
        for j in range(STW):
            ti = st2 * STW + j
            ln_pages[ti] = ln_stats(lambda ti=ti: xres[:, ti, 0:512],
                                    lambda ti=ti: xres[:, ti, 512:1024],
                                    [xres_tok[ti]])

    def emit_LNapply(st2):
        nonlocal lnt_guard
        lnT_cp = None
        for j in range(STW):
            ti = st2 * STW + j
            p, nmtok = ln_pages[ti]
            ap = SC(lambda p=p, ti=ti: nc.scalar.activation(
                ln_t, xres[:, ti, :], AF.Identity, bias=mv("nmr", p)[:, 0:1],
                scale=mv("rstd", p)[:, 0:1]), waits=[nmtok, lnt_guard, mixed_guard])
            ln_guard[p] = ap
            lnt_guard, lnT_cp = transposes8(
                lambda c: rv_(RO_A + AO_ln + c * 128, 128),
                lambda rnd, j=j: lnT[:, rnd * 4:(rnd + 1) * 4, j * 128:(j + 1) * 128],
                [ap] + ([lnT_free[0]] if (j == 0 and lnT_free[0]) else []))
        if st2 == 0:
            tap("lnT0", lambda: rv_(RO_A + AO_lnT, 2048).bitcast(F32), [128, 2048],
                [lnT_cp])
        lnT_box[0] = lnT_cp

    for st in range(NST):
        if st == 0:
            emit_LNstats(0)
        emit_LNapply(st)
        lnT_cp = lnT_box[0]
        rst['ph'] = f'proj{st}'
        for mg in range(6):
            slots = [ring_load(lambda k=k, mg=mg:
                               d["qkv_w"][k * 128:(k + 1) * 128, mg * 512:(mg + 1) * 512])
                     for k in range(8)]
            if mg < 4:
                for m in range(4):
                    ch = mg * 4 + m
                    bk = pbk[m % 2]
                    bw = bank_wait(bk) + [lnT_cp]
                    mm = None
                    for k in range(8):
                        s_, stk = slots[k]
                        mm = T(lambda bk=bk, s_=s_, m=m, k=k: nc.tensor.matmul(
                            bk[:, 0:256], slot(s_)[:, m * 128:(m + 1) * 128],
                            lnT[:, k, :], start=(k == 0), stop=(k == 7)),
                            waits=(bw if k == 0 else []) + [stk], tok=(k == 7))
                    if mg < 2:
                        ev = SC(lambda bk=bk, ch=ch: nc.scalar.activation(
                            qT[:, ch, :], bk[:, 0:256], AF.Identity,
                            bias=mv("qkbT")[:, ch:ch + 1], scale=1.0), waits=[mm])
                    else:
                        SC(lambda bk=bk, ch=ch: nc.scalar.activation(
                            kT[0:64, 2 * (ch - 8), 128:384], bk[0:64, 0:256],
                            AF.Identity,
                            bias=mv("qkbT")[0:64, ch:ch + 1], scale=1.0),
                           waits=[mm, halo_guard])
                        ev = SC(lambda bk=bk, ch=ch: nc.scalar.activation(
                            kT[64:128, 2 * (ch - 8) + 1, 128:384], bk[64:128, 0:256],
                            AF.Identity,
                            bias=mv("qkbT")[64:128, ch:ch + 1], scale=1.0))
                    bank_close(bk, ev)
                    if m == 3:
                        ring_release(slots, mm)
            else:
                for j2 in range(STW):
                    bk = pbk[2 + j2]
                    bw = bank_wait(bk) + [lnT_cp, halo_guard]
                    for k in range(8):
                        s_, stk = slots[k]
                        T(lambda bk=bk, s_=s_, j2=j2, k=k: nc.tensor.matmul(
                            bk[:, 0:512], lnT[:, k, j2 * 128:(j2 + 1) * 128],
                            slot(s_)[:, 0:512], start=(k == 0), stop=False),
                            waits=(bw if k == 0 else []) + [stk])
                    mm = T(lambda bk=bk, mg=mg: nc.tensor.matmul(
                        bk[:, 0:512], ONES[0:1, 0:128],
                        BIASP[0:1, (mg - 4) * 512:(mg - 3) * 512],
                        start=False, stop=True), tok=True)
                    ev = SC(lambda bk=bk, j2=j2, mg=mg: nc.scalar.copy(
                        vV65[:, 1 + j2, (mg - 4) * 8:(mg - 3) * 8, 0:64],
                        bk[:, 0:512].rearrange("p (h e) -> p h e", h=8)),
                        waits=[mm])
                    b0 = tbk[0].bitcast(F32)
                    bw0 = bank_wait(tbk[0])
                    vgm = V(lambda bk=bk, mg=mg, b0=b0: nc.vector.tensor_tensor(
                        b0[:, 0:512], bk[:, 0:512],
                        glwB[:, (mg - 4) * 512:(mg - 3) * 512], ALU.mult),
                        waits=bw0 + [mm, ev])
                    bank_close(bk, vgm)
                    def _vgs(j2=j2, mg=mg, b0=b0):
                        with nc.allow_low_precision(reason="f32r bits are fp32"):
                            return nc.vector.reduce_sum(
                                vV65[:, 1 + j2, (mg - 4) * 8:(mg - 3) * 8, 65:66],
                                b0[:, 0:512].rearrange("p (h e) -> p h e", h=8),
                                axis=AX.X)
                    vgr = V(_vgs)
                    bank_close(tbk[0], vgr)
                    if j2 == STW - 1:
                        ring_release(slots, mm)
        slots = [ring_load(lambda k=k: d["rq_w"][k * 128:(k + 1) * 128, :], cols=256)
                 for k in range(8)]
        for m in range(2):
            bk = pbk[4 + m]
            bw = bank_wait(bk) + [lnT_cp]
            mm = None
            for k in range(8):
                s_, stk = slots[k]
                mm = T(lambda bk=bk, s_=s_, m=m, k=k: nc.tensor.matmul(
                    bk[:, 0:256], slot(s_, 256)[:, m * 128:(m + 1) * 128],
                    lnT[:, k, :], start=(k == 0), stop=(k == 7)),
                    waits=(bw if k == 0 else []) + [stk], tok=(k == 7))
            ev = SC(lambda bk=bk, m=m: nc.scalar.activation(
                rqT[:, m, :], bk[:, 0:256], AF.Identity,
                bias=mv("rqbT")[:, m:m + 1], scale=1.0), waits=[mm])
            bank_close(bk, ev)
            if m == 1:
                ring_release(slots, mm)
                lnT_free[0] = mm
        proj_done = P.bar("scalar")
        if st == 0:
            tap("qT0", lambda: rv_(RO_A + AO_qT, 2048).bitcast(F32), [128, 2048],
                [proj_done])
            tap("kT0", lambda: rv_(RO_A + AO_kT, 3072).bitcast(F32), [128, 3072],
                [proj_done])
            tap("v0", lambda: rv_(RO_A + AO_v, 3072).bitcast(F32), [128, 3072],
                [proj_done])

        lv_ = lambda ti: fv_(FO_B + (ti % 2) * 1024, 1024)
        rv2_ = lambda ti: fv_(FO_B + 2048 + (ti % 2) * 1024, 1024)
        tistate = {}
        def stP1(j):
            ti = st * STW + j
            pt = ti % 2
            maskap = maskA0_t if ti == 0 else maskA_t
            av_banks = [pbk[4], pbk[5], tbk[1].bitcast(F32)]
            av_w = [bank_wait(b) for b in av_banks]
            av_last = [None, None, None]
            for hg in range(8):
                hgp = hg % 2
                bk = pbk[hgp]
                bw = bank_wait(bk) + [proj_done, halo_guard]
                for c in range(2):
                    for hh in range(2):
                        h = hg * 2 + hh
                        T(lambda bk=bk, c=c, hh=hh, h=h, hg=hg, j=j:
                          nc.tensor.matmul(
                            bk[:, (c * 2 + hh) * 128:(c * 2 + hh + 1) * 128],
                            Rb[:, RO_A + AO_kT + h * 384 + (j + c) * 128:
                               RO_A + AO_kT + h * 384 + (j + c + 1) * 128],
                            Rb[:, RO_A + AO_qT + hg * 256 + j * 128:
                               RO_A + AO_qT + hg * 256 + (j + 1) * 128],
                            start=(c == 0 and hh == 0), stop=False),
                            waits=(bw if (c == 0 and hh == 0) else []))
                mm = None
                for c in range(2):
                    mk = (maskA0_t[:, 0:256] if (ti == 0 and c == 0)
                          else maskA_t[:, c * 256:(c + 1) * 256])
                    mm = T(lambda bk=bk, c=c, mk=mk: nc.tensor.matmul(
                        bk[:, c * 256:(c + 1) * 256], IDENT, mk,
                        start=False, stop=(c == 1)), tok=(c == 1))
                ex_last = None
                for c in range(2):
                    ex_last = SC(lambda bk=bk, c=c, hgp=hgp: nc.scalar.activation(
                        attnT_sb[hgp][:, c * 256:(c + 1) * 256],
                        bk[:, c * 256:(c + 1) * 256], AF.Exp,
                        bias=0.0, scale=0.125),
                        waits=([mm] + ([attnT_guard[hgp]] if c == 0 else [])))
                bank_close(bk, ex_last)
                avm = None
                for hh in range(2):
                    h = hg * 2 + hh
                    bi = 0 if h < 7 else (1 if h < 14 else 2)
                    co = (h - (0, 7, 14)[bi]) * 68
                    for c in range(2):
                        stf = (h in (0, 7, 14)) and c == 0
                        spf = (h in (6, 13, 15)) and c == 1
                        w = ([ex_last] if (hh == 0 and c == 0) else [])
                        if stf:
                            w = w + av_w[bi]
                        avm = T(lambda bb=av_banks[bi], co=co, c=c, hh=hh,
                                h=h, j=j, hgp=hgp, stf=stf, spf=spf:
                                nc.tensor.matmul(
                            bb[:, co:co + 68],
                            attnT_sb[hgp][:, (c * 2 + hh) * 128:
                                          (c * 2 + hh + 1) * 128],
                            vV65[:, j + c, h, 0:68],
                            start=stf, stop=spf),
                            waits=w, tok=(spf or (hh == 1 and c == 1)))
                        if spf:
                            av_last[bi] = avm
                attnT_guard[hgp] = avm
            for bi, (h0, h1) in enumerate([(0, 7), (7, 14), (14, 16)]):
                src = av_banks[bi][:, 0:(h1 - h0) * 68]
                V(lambda src=src, pt=pt, h0=h0, h1=h1: nc.vector.tensor_copy(
                    mv("sums", pt)[:, h0:h1].unsqueeze(2),
                    src.rearrange("p (h e) -> p h e", e=68)[:, :, 64:65]),
                    waits=[av_last[bi]])
            if ti == 0:
                V(lambda pt=pt: nc.vector.tensor_scalar(
                    mv("sums", pt), mv("sums", pt), mv("row0add")[:, 0:1], None,
                    ALU.add))
            rc = V(lambda pt=pt: nc.vector.reciprocal(mv("recips", pt),
                                                      mv("sums", pt)))
            lg_tok = None
            for bi, (h0, h1) in enumerate([(0, 7), (7, 14), (14, 16)]):
                src = av_banks[bi][:, 0:(h1 - h0) * 68]
                lg_tok = V(lambda src=src, pt=pt, h0=h0, h1=h1:
                           nc.vector.tensor_tensor(
                    mv("lg", pt)[:, h0:h1].unsqueeze(2),
                    src.rearrange("p (h e) -> p h e", e=68)[:, :, 65:66],
                    mv("recips", pt)[:, h0:h1].unsqueeze(2), ALU.mult))
            tistate[(j, 'lg')] = lg_tok
            lv_toks = []
            for bi, (h0, h1) in enumerate([(0, 7), (7, 14), (14, 16)]):
                src = av_banks[bi][:, 0:(h1 - h0) * 68]
                lvt = V(lambda src=src, pt=pt, h0=h0, h1=h1, ti=ti:
                        nc.vector.tensor_tensor(
                    fv_(FO_B + (ti % 2) * 1024 + h0 * 64, (h1 - h0) * 64).rearrange(
                        "p (h e) -> p h e", e=64),
                    src.rearrange("p (h e) -> p h e", e=68)[:, :, 0:64],
                    mv("recips", pt)[:, h0:h1].unsqueeze(2)
                    .broadcast_to([128, h1 - h0, 64]), ALU.mult),
                    waits=[av_last[bi], rc])
                bank_close(av_banks[bi], lvt)
                lv_toks.append(lvt)
            if ti == 0:
                tap("local0", lambda: lv_(ti), [128, 1024], [lv_toks[-1]])
            if ti == 1:
                tap("local1", lambda: lv_(ti), [128, 1024], [lv_toks[-1]])
            if ti == 2:
                tap("local2", lambda ti=ti: lv_(ti), [128, 1024], [lv_toks[-1]])
                tap("sums1", lambda: mv("sums", 1), [128, 16], [rc])

            tistate[(j, 'lv')] = lv_toks
        def stP2(j):
            nonlocal lat_guard
            ti = st * STW + j
            pt = ti % 2
            maskap = maskA0_t if ti == 0 else maskA_t
            if KSUB < 2:
                return
            bk = pbk[2]
            bw = bank_wait(bk) + [rk_tok]
            mm = None
            for lh in range(4):
                r0 = (lh % 2) * 64
                ch = lh // 2
                mm = T(lambda bk=bk, lh=lh, ch=ch, j=j: nc.tensor.matmul(
                    bk[:, lh * 64:(lh + 1) * 64],
                    Rb[:, RO_A + AO_rqT + ch * 256 + j * 128:
                       RO_A + AO_rqT + ch * 256 + (j + 1) * 128].bitcast(F32),
                    Rb[:, RO_rkT + lh * 64:RO_rkT + (lh + 1) * 64].bitcast(F32),
                    start=(lh == 0), stop=(lh == 3)),
                    waits=(bw if lh == 0 else []), tok=(lh == 3))
            vsc = V(lambda bk=bk: nc.vector.tensor_copy(
                mv("latsc"), bk[:, 0:256]), waits=[mm, lat_guard])
            bank_close(bk, vsc)
            vsc = V(lambda pt=pt, ti=ti: nc.vector.tensor_tensor(
                mv("latsc").rearrange("p (l k) -> p l k", l=4),
                mv("latsc").rearrange("p (l k) -> p l k", l=4),
                latmaskB[:, ti * 64:(ti + 1) * 64].unsqueeze(1)
                .broadcast_to([128, 4, 64]), ALU.add))
            V(lambda pt=pt: nc.vector.reduce_max(
                mv("negmaxL", pt), mv("latsc").rearrange("p (l k) -> p l k", l=4),
                axis=AX.X, negate=True))
            vbl = V(lambda pt=pt: nc.vector.tensor_scalar(
                mv("biasL", pt), mv("negmaxL", pt), 0.125, None, ALU.mult))
            exl = None
            for lh in range(4):
                exl = SC(lambda lh=lh, pt=pt: nc.scalar.activation(
                    rv_(RO_A + AO_latexp + lh * 64, 64),
                    mv("latsc")[:, lh * 64:(lh + 1) * 64], AF.Exp,
                    bias=mv("biasL", pt)[:, lh:lh + 1], scale=0.125,
                    accum_out=mv("latsums", pt)[:, lh:lh + 1]),
                    waits=([vbl] if lh == 0 else []))
            if KLAT < 2:
                return
            lat_guard = exl
            bt = tbk[0]
            btw = bank_wait(bt) + [exl]
            tt = None
            for lh in range(4):
                tt = T(lambda bt=bt, lh=lh: nc.tensor.transpose(
                    bt[0:64, lh * 128:(lh + 1) * 128],
                    rv_(RO_A + AO_latexp + lh * 64, 64), IDENT),
                    waits=(btw if lh == 0 else []), tok=(lh == 3))
            cp = V(lambda bt=bt: nc.vector.tensor_copy(
                Rb[0:64, RO_A + AO_latT:RO_A + AO_latT + 512], bt[0:64, 0:512]),
                waits=[tt])
            bank_close(bt, cp)
            lr = V(lambda pt=pt: nc.vector.reciprocal(mv("latrecip", pt),
                                                      mv("latsums", pt)))
            if ti == 0:
                lr = V(lambda pt=pt: nc.vector.tensor_scalar(
                    mv("latrecip", pt), mv("latrecip", pt), mv("latrow0")[:, 0:1],
                    None, ALU.mult))
            if KLAT < 3:
                return
            bk = pbk[3]
            bw = bank_wait(bk) + [cp, rvv_tok]
            mm = None
            for lh in range(4):
                mm = T(lambda bk=bk, lh=lh: nc.tensor.matmul(
                    bk[:, lh * 64:(lh + 1) * 64],
                    Rb[0:64, RO_A + AO_latT + lh * 128:
                       RO_A + AO_latT + (lh + 1) * 128].bitcast(F32),
                    Rb[0:64, RO_rv + lh * 64:RO_rv + (lh + 1) * 64].bitcast(F32),
                    start=(lh == 0), stop=(lh == 3)),
                    waits=(bw if lh == 0 else []), tok=(lh == 3))
            vlo = V(lambda bk=bk, pt=pt: nc.vector.tensor_tensor(
                rv_(RO_A + AO_latexp, 256).rearrange("p (l e) -> p l e", l=4),
                bk[:, 0:256].rearrange("p (l e) -> p l e", l=4),
                mv("latrecip", pt).unsqueeze(2).broadcast_to([128, 4, 64]), ALU.mult),
                waits=[mm, lr])
            bank_close(bk, vlo)
            if KLAT < 4:
                return
            bt = tbk[0]
            btw = bank_wait(bt) + [vlo]
            tt = None
            for c in range(2):
                tt = T(lambda bt=bt, c=c: nc.tensor.transpose(
                    bt[:, c * 128:(c + 1) * 128],
                    rv_(RO_A + AO_latexp + c * 128, 128), IDENT),
                    waits=(btw if c == 0 else []), tok=(c == 1))
            cp2 = V(lambda bt=bt: nc.vector.tensor_copy(
                rv_(RO_A + AO_latoutT, 256), bt[:, 0:256]), waits=[tt])
            bank_close(bt, cp2)
            if KLAT < 5:
                return
            bgw = bank_wait(tbk[0]) + [cp2]
            for c in range(2):
                T(lambda c=c: nc.tensor.matmul(
                    tbk[0].bitcast(F32)[:, 0:16],
                    rv_(RO_A + AO_latoutT + c * 128, 128), rwg[:, c * 16:(c + 1) * 16],
                    start=(c == 0), stop=False), waits=(bgw if c == 0 else []))
            mmrg = T(lambda: nc.tensor.matmul(
                tbk[0].bitcast(F32)[:, 0:16], ONES[0:1, 0:128],
                mv("routbg")[0:1, 0:16], start=False, stop=True), tok=True)
            rg_tok = V(lambda pt=pt: nc.vector.tensor_copy(
                mv("rg", pt), tbk[0].bitcast(F32)[:, 0:16]), waits=[mmrg])
            bank_close(tbk[0], rg_tok)
            tistate[(j, 'rg')] = rg_tok
            rem_toks = []
            for half in range(2):
                bk = pbk[2 + half]
                bw = bank_wait(bk) + ([cp2] if half == 0 else [])
                for c in range(2):
                    T(lambda bk=bk, c=c, half=half: nc.tensor.matmul(
                        bk[:, 0:512],
                        rv_(RO_A + AO_latoutT + c * 128, 128),
                        routw[:, c, half * 512:(half + 1) * 512],
                        start=(c == 0), stop=False), waits=(bw if c == 0 else []))
                mm = T(lambda bk=bk, half=half: nc.tensor.matmul(
                    bk[:, 0:512], ONES[0:1, 0:128],
                    mv("routbB")[0:1, half * 512:(half + 1) * 512],
                    start=False, stop=True), tok=True)
                ev = V(lambda bk=bk, half=half: nc.vector.tensor_copy(
                    fv_(FO_B + 2048 + (ti % 2) * 1024 + half * 512, 512), bk[:, 0:512]),
                    waits=[mm, mixed_guard])
                bank_close(bk, ev)
                rem_toks.append(ev)
            if ti == 0:
                tap("remote0", lambda: rv2_(ti), [128, 1024], [rem_toks[-1]])
            if ti == 1:
                tap("remote1", lambda: rv2_(ti), [128, 1024], [rem_toks[-1]])
            if ti == 2:
                tap("remote2", lambda ti=ti: rv2_(ti), [128, 1024], [rem_toks[-1]])

            tistate[(j, 'rem')] = rem_toks
        def stG4(j):
            nonlocal mixed_guard
            ti = st * STW + j
            pt = ti % 2
            maskap = maskA0_t if ti == 0 else maskA_t
            lv_toks = tistate[(j, 'lv')]
            rem_toks = tistate[(j, 'rem')]
            if KSUB < 3:
                return
            vgs = V(lambda pt=pt: nc.vector.tensor_tensor(
                mv("lg", pt), mv("lg", pt), mv("rg", pt), ALU.add),
                waits=[tistate[(j, 'lg')], tistate[(j, 'rg')]])
            sg = SC(lambda pt=pt: nc.scalar.activation(
                mv("gate", pt), mv("lg", pt), AF.Sigmoid, bias=0.0, scale=1.0),
                waits=[vgs])
            V(lambda: nc.vector.tensor_tensor(lv_(ti), lv_(ti), rv2_(ti),
                                              ALU.subtract),
              waits=list(lv_toks) + list(rem_toks))
            V(lambda pt=pt: nc.vector.tensor_tensor(
                lv_(ti).rearrange("p (h e) -> p h e", h=16),
                lv_(ti).rearrange("p (h e) -> p h e", h=16),
                mv("gate", pt).unsqueeze(2).broadcast_to([128, 16, 64]), ALU.mult),
                waits=[sg])
            vmx = V(lambda: nc.vector.tensor_tensor(mixed_v, rv2_(ti), lv_(ti),
                                                    ALU.add),
                    waits=[lnt_guard, mixed_guard])
            if ti == 0:
                tap("mixed0", lambda: mixed_v.bitcast(F32), [128, 1024], [vmx])
            if ti == 1:
                tap("mixed1", lambda: mixed_v.bitcast(F32), [128, 1024], [vmx])
            if ti == 2:
                tap("mixed2", lambda: mixed_v.bitcast(F32), [128, 1024], [vmx])

            if KSUB < 4:
                return
            mtt, mcp = transposes8(
                lambda c: mixed_v[:, c * 128:(c + 1) * 128],
                lambda rnd: rv_(RO_A + AO_mixedT, 1024).rearrange(
                    "p (c t) -> p c t", c=8)[:, rnd * 4:(rnd + 1) * 4, :],
                [vmx])
            mixed_guard = mtt
            ao_mm = []
            for half in range(2):
                oslots = [ring_load(
                    lambda k=k, half=half:
                    d["out_w"][k * 128:(k + 1) * 128, half * 512:(half + 1) * 512])
                    for k in range(8)]
                bk = pbk[half]
                bw = bank_wait(bk) + [mcp]
                for k in range(8):
                    s_, stk = oslots[k]
                    T(lambda bk=bk, s_=s_, k=k: nc.tensor.matmul(
                        bk[:, 0:512],
                        rv_(RO_A + AO_mixedT + k * 128, 128), slot(s_)[:, 0:512],
                        start=(k == 0), stop=False),
                        waits=(bw if k == 0 else []) + [stk])
                mm = T(lambda bk=bk, half=half: nc.tensor.matmul(
                    bk[:, 0:512], ONES[0:1, 0:128],
                    Fb[0:1, FO_B + BO_outb + half * 512:FO_B + BO_outb + (half + 1) * 512],
                    start=False, stop=True), tok=True)
                ring_release(oslots, mm)
                ao_mm.append(mm)
            p, nmtok = ln_stats(lambda: pbk[0][:, 0:512], lambda: pbk[1][:, 0:512],
                                ao_mm)
            va = V(lambda pt2=p, ti=ti: nc.vector.tensor_scalar(
                xres[:, ti, :], xres[:, ti, :], mv("nmr", pt2)[:, 0:1], None,
                ALU.add), waits=[nmtok])
            for half in range(2):
                va = V(lambda bk=pbk[half], p2=p, ti=ti, half=half:
                       nc.vector.scalar_tensor_tensor(
                    xres[:, ti, half * 512:(half + 1) * 512], bk[:, 0:512],
                    mv("rstd", p2)[:, 0:1],
                    xres[:, ti, half * 512:(half + 1) * 512],
                    ALU.mult, ALU.add))
                bank_close(pbk[half], va)
            ln_guard[p] = va
            xres_tok[ti] = va
            if ti == 0:
                tap("y1_0", lambda: xres[:, 0, :], [128, 1024], [va])

        if st < KATT:
            stP1(0); stP2(0); stP1(1)
            if st < NST - 1:
                emit_LNstats(st + 1)
            stG4(0); stP2(1); stG4(1)
        elif st < NST - 1:
            emit_LNstats(st + 1)
        if st < NST - 1 and st < KATT:
            hg1 = G(lambda: nc.gpsimd.tensor_copy(
                rv_(RO_A + AO_kT, 6144).rearrange(
                    "p (c t) -> p c t", c=16)[:, :, 0:128],
                rv_(RO_A + AO_kT, 6144).rearrange(
                    "p (c t) -> p c t", c=16)[:, :, 256:384]),
                waits=[P.bar("vector"), attnT_guard[0], attnT_guard[1], mixed_guard])
            halo_guard = G(lambda: nc.gpsimd.tensor_copy(
                vVf[:, 0, :], vVf[:, 2, :]), waits=[hg1])

    tap("y1all", lambda: fv_(FO_xres, NT * 1024), [128, NT * 1024],
        [xres_tok[i] for i in range(NT)])

    attn_end = (t_ev, t_ev.n)
    for k in range(RING_X):
        rst["free"][RING_N + k] = attn_end
    rst["active"] = RING_N + RING_X
    GD(lambda: nc.gpsimd.dma_start(
        out=Rb[0:16, RO_A + 12288:RO_A + 12288 + 1024], in_=d["be2"]),
        waits=[attn_end])
    be2_tok = (g_ld, g_ld.n)

    if KPH < 3:
        for ti in range(NT):
            P.emit("gpsimd", (lambda ti=ti: nc.gpsimd.dma_start(
                out=yout[ti * 128:(ti + 1) * 128, :], in_=xres[:, ti, :])),
                waits=[xres_tok[ti]], inc=[so], selfser=False)
        P.emit("gpsimd", lambda: nc.gpsimd.engine_nop(), waits=[(so, 16 * NT)],
               selfser=False)
        for s in P.sems:
            s.h = nc.alloc_semaphore(s.name)
        with nc.Block() as block:
            def mk1(eng):
                def run(proxy):
                    hw = {}
                    for waits, build, incs in P.events[eng]:
                        for sm, v in waits:
                            if hw.get(id(sm), 0) < v:
                                proxy.wait_ge(sm.h, v)
                                hw[id(sm)] = v
                        ins = build()
                        for sm in incs:
                            ins.then_inc(sm.h, sm.step)
                return run
            block.tensor(mk1("tensor"))
            block.vector(mk1("vector"))
            block.scalar(mk1("scalar"))
            block.gpsimd(mk1("gpsimd"))
            block.sync(mk1("sync"))
        return nc, taps

    ln2_guard = None
    yo_guard = None

    for Hf in range(2):
        ln2T_cp = None
        comb_toks = []
        for jj in range(8):
            ti = Hf * 8 + jj
            pt = jj % 2
            p, nmtok = ln_stats(lambda ti=ti: xres[:, ti, 0:512],
                                lambda ti=ti: xres[:, ti, 512:1024],
                                [xres_tok[ti]])
            ap = SC(lambda p=p, ti=ti: nc.scalar.activation(
                ln2_t, xres[:, ti, :], AF.Identity, bias=mv("nmr", p)[:, 0:1],
                scale=mv("rstd", p)[:, 0:1]), waits=[nmtok, ln2_guard])
            ln_guard[p] = ap
            ln2_guard, ln2T_cp = transposes8(
                lambda c: rv_(RO_attnsb + c * 128, 128),
                lambda rnd, jj=jj: ln2T[:, rnd * 4:(rnd + 1) * 4,
                                        jj * 128:(jj + 1) * 128],
                [ap])
            bk = pbk[2]
            bw = bank_wait(bk) + [ln2T_cp, gwT_tok]
            mm = None
            for c in range(8):
                mm = T(lambda bk=bk, c=c, jj=jj: nc.tensor.matmul(
                    bk[:, 0:16],
                    rv_(RO_A + AO_ln2T + c * 1024 + jj * 128, 128).bitcast(F32),
                    rv_(RO_gwT + c * 16, 16).bitcast(F32),
                    start=(c == 0), stop=(c == 7)),
                    waits=(bw if c == 0 else []), tok=(c == 7))
            V(lambda bk=bk, pt=pt: nc.vector.reduce_max(
                mv("scr1a", pt), bk[:, 0:16], axis=AX.X, negate=True), waits=[mm])
            exr = SC(lambda bk=bk, pt=pt: nc.scalar.activation(
                mv("probs", pt), bk[:, 0:16], AF.Exp, bias=mv("scr1a", pt)[:, 0:1],
                scale=1.0, accum_out=mv("scr1b", pt)[:, 0:1]),
                waits=[P.bar("vector")])
            bank_close(bk, exr)
            V(lambda pt=pt: nc.vector.reciprocal(mv("scr1a", pt), mv("scr1b", pt)),
              waits=[exr])
            V(lambda pt=pt: nc.vector.tensor_scalar(
                mv("probs", pt), mv("probs", pt), mv("scr1a", pt)[:, 0:1], None,
                ALU.mult))
            V(lambda pt=pt: nc.vector.tensor_tensor(mv("s2", pt), mv("probs", pt),
                                                    mv("gbB"), ALU.add))
            V(lambda pt=pt: nc.vector.reduce_max(
                mv("scr4a", pt), mv("s2", pt).rearrange("p (g e) -> p g e", g=4),
                axis=AX.X))
            V(lambda pt=pt: nc.vector.tensor_tensor(
                mv("scr16a", pt).rearrange("p (g e) -> p g e", g=4),
                mv("s2", pt).rearrange("p (g e) -> p g e", g=4),
                mv("scr4a", pt).unsqueeze(2).broadcast_to([128, 4, 4]), ALU.is_equal))
            V(lambda pt=pt: nc.vector.scalar_tensor_tensor(
                mv("scr16b", pt), mv("scr16a", pt), NEG, mv("s2", pt),
                ALU.mult, ALU.add))
            V(lambda pt=pt: nc.vector.reduce_max(
                mv("scr4b", pt), mv("scr16b", pt).rearrange("p (g e) -> p g e", g=4),
                axis=AX.X))
            V(lambda pt=pt: nc.vector.tensor_tensor(
                mv("scr4a", pt), mv("scr4a", pt), mv("scr4b", pt), ALU.add))
            V(lambda pt=pt: nc.vector.reduce_max(mv("scr1a", pt), mv("scr4a", pt),
                                                 axis=AX.X))
            V(lambda pt=pt: nc.vector.tensor_tensor(
                mv("scr4b", pt), mv("scr4a", pt),
                mv("scr1a", pt).broadcast_to([128, 4]), ALU.is_equal))
            V(lambda pt=pt: nc.vector.scalar_tensor_tensor(
                mv("scr4c", pt), mv("scr4b", pt), NEG, mv("scr4a", pt),
                ALU.mult, ALU.add))
            V(lambda pt=pt: nc.vector.reduce_max(mv("scr1b", pt), mv("scr4c", pt),
                                                 axis=AX.X))
            V(lambda pt=pt: nc.vector.tensor_tensor(
                mv("scr4c", pt), mv("scr4c", pt),
                mv("scr1b", pt).broadcast_to([128, 4]), ALU.is_equal))
            V(lambda pt=pt: nc.vector.tensor_tensor(
                mv("scr4b", pt), mv("scr4b", pt), mv("scr4c", pt), ALU.add))
            V(lambda pt=pt: nc.vector.scalar_tensor_tensor(
                mv("scr16b", pt).rearrange("p (g e) -> p g e", g=4),
                mv("scr4b", pt).unsqueeze(2).broadcast_to([128, 4, 4]), 1.0,
                mv("zerop").rearrange("p (g e) -> p g e", g=4), ALU.mult, ALU.add))
            V(lambda pt=pt: nc.vector.memset(mv("scr16a", pt), NEG))
            V(lambda pt=pt: nc.vector.copy_predicated(
                mv("scr16a", pt), mv("scr16b", pt).bitcast(mybir.dt.int32),
                mv("s2", pt)))
            V(lambda pt=pt: nc.vector.reduce_max(mv("scr1a", pt), mv("scr16a", pt),
                                                 axis=AX.X))
            V(lambda pt=pt: nc.vector.tensor_tensor(
                mv("scr16b", pt), mv("scr16a", pt),
                mv("scr1a", pt).broadcast_to([128, 16]), ALU.is_equal))
            V(lambda pt=pt: nc.vector.scalar_tensor_tensor(
                mv("scr16a", pt), mv("scr16b", pt), 2 * NEG, mv("scr16a", pt),
                ALU.mult, ALU.add))
            V(lambda pt=pt: nc.vector.reduce_max(mv("scr1b", pt), mv("scr16a", pt),
                                                 axis=AX.X))
            V(lambda pt=pt: nc.vector.tensor_tensor(
                mv("scr16a", pt), mv("scr16a", pt),
                mv("scr1b", pt).broadcast_to([128, 16]), ALU.is_equal))
            V(lambda pt=pt: nc.vector.tensor_tensor(
                mv("scr16b", pt), mv("scr16b", pt), mv("scr16a", pt), ALU.add))
            cmb = V(lambda pt=pt, jj=jj: nc.vector.tensor_tensor(
                Rb[:, RO_A + 14848 + jj * 16:RO_A + 14848 + (jj + 1) * 16],
                mv("scr16b", pt), mv("probs", pt), ALU.mult))
            comb_toks.append(cmb)
            if ti == 0:
                tap("probs0", lambda: mv("probs", 0), [128, 16], [cmb])
                tap("comb0", lambda: Rb[:, RO_A + 14848:RO_A + 14976].bitcast(F32)[:, 0:16], [128, 16], [cmb])
                tap("r_s2", lambda: mv("s2", 0), [128, 16], [cmb])
                tap("r_gsc", lambda: mv("scr4a", 0), [128, 4], [cmb])
                tap("r_keep", lambda: mv("scr4b", 0), [128, 4], [cmb])
                tap("r_osum", lambda: mv("scr16b", 0), [128, 16], [cmb])
            bt = tbk[0]
            btw = bank_wait(bt) + [cmb]
            tt = T(lambda bt=bt, jj=jj: nc.tensor.transpose(
                bt[0:16, 0:128].bitcast(F32), Rb[:, RO_A + 14848:RO_A + 14976].bitcast(F32)[:, jj * 16:(jj + 1) * 16],
                IDENT.bitcast(F32)), waits=btw, tok=True)
            ctv = V(lambda bt=bt: nc.vector.tensor_copy(
                Fb[0:16, misc["combT"][0]:misc["combT"][0] + 128],
                bt[0:16, 0:128].bitcast(F32)), waits=[tt])
            bank_close(bt, ctv)
            for half in range(2):
                bk = pbk[4 + half]
                bw = bank_wait(bk) + [ctv]
                T(lambda bk=bk, half=half: nc.tensor.matmul(
                    bk[:, 0:512], Fb[0:16, misc["combT"][0]:misc["combT"][0] + 128],
                    Rb[0:16, RO_A + 12288 + half * 512:
                       RO_A + 12288 + (half + 1) * 512].bitcast(F32),
                    start=True, stop=False), waits=bw + [be2_tok])
                mm = T(lambda bk=bk, half=half: nc.tensor.matmul(
                    bk[:, 0:512], ONES[0:1, 0:128],
                    BIASP[0:1, 1024 + half * 512:1024 + (half + 1) * 512],
                    start=False, stop=True), tok=True)
                ev = V(lambda bk=bk, jj=jj, half=half: nc.vector.tensor_copy(
                    y_sb[:, jj, half * 512:(half + 1) * 512], bk[:, 0:512]),
                    waits=[mm])
                bank_close(bk, ev)
            if ti == 0:
                tap("yb0", lambda: y_sb[:, 0, :], [128, 1024], [ev])

        tap("ln2T0" if Hf == 0 else "ln2T1",
            lambda: rv_(RO_A + AO_ln2T, 8192).bitcast(F32), [128, 8192], [ln2T_cp])
        tap_blk = []
        for e in range(18):
            hi = e - 16
            if e < 16:
                W1 = lambda k, e=e: d["We1"][e, k * 128:(k + 1) * 128, :]
                W3 = lambda k, e=e: d["We3"][e, k * 128:(k + 1) * 128, :]
                W2 = lambda c, oh, e=e: d["We2"][e, c * 128:(c + 1) * 128,
                                                 oh * 512:(oh + 1) * 512]
                b1 = lambda m, e=e: mv("be1T")[:, m * 16 + e:m * 16 + e + 1]
                b3 = lambda m, e=e: mv("be3T")[:, m * 16 + e:m * 16 + e + 1]
            else:
                W1 = lambda k, hi=hi: d["ws1"][k * 128:(k + 1) * 128,
                                               hi * 512:(hi + 1) * 512]
                W3 = lambda k, hi=hi: d["ws3"][k * 128:(k + 1) * 128,
                                               hi * 512:(hi + 1) * 512]
                W2 = lambda c, oh, hi=hi: d["ws2"][hi * 512 + c * 128:
                                                   hi * 512 + (c + 1) * 128,
                                                   oh * 512:(oh + 1) * 512]
                b1 = lambda m, hi=hi: mv("bs1T")[:, hi * 4 + m:hi * 4 + m + 1]
                b3 = lambda m, hi=hi: mv("bs3T")[:, hi * 4 + m:hi * 4 + m + 1]

            slots1 = [ring_load(lambda k=k, W=W1: W(k)) for k in range(8)]
            h_toks = {}
            for g2 in range(2):
                for m in range(4):
                    bk = pbk[m]
                    bw = bank_wait(bk) + [ln2T_cp]
                    mm = None
                    for k in range(8):
                        s_, stk = slots1[k]
                        mm = T(lambda bk=bk, s_=s_, m=m, k=k, g2=g2:
                               nc.tensor.matmul(
                            bk[:, 0:512], slot(s_)[:, m * 128:(m + 1) * 128],
                            ln2T[:, k, g2 * 512:(g2 + 1) * 512],
                            start=(k == 0), stop=(k == 7)),
                            waits=(bw if k == 0 else []) + [stk], tok=(k == 7))
                    ev = SC(lambda bk=bk, g2=g2, m=m, b1=b1: nc.scalar.activation(
                        hV[:, g2, m, :], bk[:, 0:512], AF.Silu,
                        bias=b1(m), scale=1.0), waits=[mm, yo_guard])
                    bank_close(bk, ev)
                    if g2 == 1 and m == 3:
                        ring_release(slots1, mm)
                    if Hf == 0 and e == 0 and g2 == 0 and m == 0:
                        tap_blk.append(tap("slot0", lambda: rv_(0, 10 * 512)
                                           .bitcast(F32), [128, 5120], [ev]))
                    if Hf == 0 and e == 0 and g2 == 0 and m == 3:
                        tap_blk.append(tap("h1s0", lambda: rv_(RO_A + AO_h, 2048)
                                           .bitcast(F32), [128, 2048], [ev]))
            slots3 = [ring_load(lambda k=k, W=W3: W(k)) for k in range(8)]
            for g2 in range(2):
                for m in range(4):
                    bk = pbk[m]
                    bw = bank_wait(bk) + [t for t in tap_blk if t]

                    mm = None
                    for k in range(8):
                        s_, stk = slots3[k]
                        mm = T(lambda bk=bk, s_=s_, m=m, k=k, g2=g2:
                               nc.tensor.matmul(
                            bk[:, 0:512], slot(s_)[:, m * 128:(m + 1) * 128],
                            ln2T[:, k, g2 * 512:(g2 + 1) * 512],
                            start=(k == 0), stop=(k == 7)),
                            waits=(bw if k == 0 else []) + [stk], tok=(k == 7))
                    ev = V(lambda bk=bk, g2=g2, m=m, b3=b3:
                           nc.vector.scalar_tensor_tensor(
                        hV[:, g2, m, :], bk[:, 0:512], b3(m), hV[:, g2, m, :],
                        ALU.add, ALU.mult), waits=[mm, P.bar("scalar")])
                    h_toks[(g2, m)] = ev
                    bank_close(bk, ev)
                    if g2 == 1 and m == 3:
                        ring_release(slots3, mm)
            slots2 = {}
            for c in range(4):
                for oh in range(2):
                    slots2[(c, oh)] = ring_load(lambda c=c, oh=oh, W=W2: W(c, oh))
            for g2 in range(2):
                for j2 in range(4):
                    jj2 = g2 * 4 + j2
                    for oh in range(2):
                        bk = pbk[4 + ((j2 * 2 + oh) % 2)]
                        bw = bank_wait(bk) + [h_toks[(g2, 3)]]
                        mm = None
                        for c in range(4):
                            s_, stk = slots2[(c, oh)]
                            mm = T(lambda bk=bk, s_=s_, c=c, g2=g2, j2=j2:
                                   nc.tensor.matmul(
                                bk[:, 0:512],
                                hV[:, g2, c, j2 * 128:(j2 + 1) * 128],
                                slot(s_)[:, 0:512],
                                start=(c == 0), stop=(c == 3)),
                                waits=(bw if c == 0 else []) + [stk], tok=(c == 3))
                        yo_guard = mm
                        if e < 16:
                            ev = V(lambda bk=bk, jj2=jj2, oh=oh, e=e:
                                   nc.vector.scalar_tensor_tensor(
                                y_sb[:, jj2, oh * 512:(oh + 1) * 512], bk[:, 0:512],
                                Rb[:, RO_A + 14848:RO_A + 14976].bitcast(F32)[:, jj2 * 16 + e:jj2 * 16 + e + 1],
                                y_sb[:, jj2, oh * 512:(oh + 1) * 512],
                                ALU.mult, ALU.add),
                                waits=[mm] + comb_toks[jj2:jj2 + 1])
                        else:
                            ev = V(lambda bk=bk, jj2=jj2, oh=oh:
                                   nc.vector.tensor_tensor(
                                y_sb[:, jj2, oh * 512:(oh + 1) * 512],
                                y_sb[:, jj2, oh * 512:(oh + 1) * 512],
                                bk[:, 0:512], ALU.add), waits=[mm])
                        bank_close(bk, ev)
            ring_release(list(slots2.values()), yo_guard)
            if Hf == 0 and e == 0 and "h0" in TAP_NAMES:
                tap("h0", lambda: rv_(RO_A + AO_h, 2048).bitcast(F32), [128, 2048],
                    [h_toks[(0, 3)]])

        for jj in range(8):
            ti = Hf * 8 + jj
            p, nmtok = ln_stats(lambda jj=jj: y_sb[:, jj, 0:512],
                                lambda jj=jj: y_sb[:, jj, 512:1024],
                                [P.bar("vector")])
            va = V(lambda p=p, ti=ti: nc.vector.tensor_scalar(
                xres[:, ti, :], xres[:, ti, :], mv("nmr", p)[:, 0:1], None, ALU.add),
                waits=[nmtok])
            va = V(lambda p=p, jj=jj, ti=ti: nc.vector.scalar_tensor_tensor(
                xres[:, ti, :], y_sb[:, jj, :], mv("rstd", p)[:, 0:1],
                xres[:, ti, :], ALU.mult, ALU.add))
            ln_guard[p] = va
            P.emit("gpsimd", (lambda ti=ti: nc.gpsimd.dma_start(
                out=yout[ti * 128:(ti + 1) * 128, :], in_=xres[:, ti, :])),
                waits=[va], inc=[so], selfser=False)
            if ti == 0:
                tap("y0", lambda: y_sb[:, 0, :], [128, 1024], [va])

    P.emit("gpsimd", lambda: nc.gpsimd.engine_nop(), waits=[(so, 16 * NT)],
           selfser=False)

    for s in P.sems:
        s.h = nc.alloc_semaphore(s.name)

    with nc.Block() as block:
        def make(eng):
            def run(proxy):
                hw = {}
                for waits, build, incs in P.events[eng]:
                    for sm, v in waits:
                        if hw.get(id(sm), 0) < v:
                            proxy.wait_ge(sm.h, v)
                            hw[id(sm)] = v
                    ins = build()
                    for sm in incs:
                        ins.then_inc(sm.h, sm.step)
            return run
        block.tensor(make("tensor"))
        block.vector(make("vector"))
        block.scalar(make("scalar"))
        block.gpsimd(make("gpsimd"))
        block.sync(make("sync"))

    return nc, taps


_CACHE = {}


def _host_inputs(core, inp):
    b, q = core // 4, core % 4
    x = np.asarray(inp["L0_attn_x"])
    g0 = q * R
    m = {}
    m["xfull"] = np.ascontiguousarray(x[b])
    m["xown"] = np.ascontiguousarray(x[b, g0:g0 + R])
    m["xhalo"] = np.ascontiguousarray(x[b, g0 - 128:g0] if q > 0 else x[b, 0:128])
    for nm, key in [("qkv_w", "L0_attn_qkv_w"), ("rq_w", "L0_attn_rq_w"),
                    ("rkv_w", "L0_attn_rkv_w"), ("rout_w", "L0_attn_rout_w"),
                    ("out_w", "L0_attn_out_w"), ("gate_w", "L0_ffn_gate_w"),
                    ("We1", "L0_ffn_We1"), ("We2", "L0_ffn_We2"),
                    ("We3", "L0_ffn_We3"), ("ws1", "L0_ffn_ws1"),
                    ("ws2", "L0_ffn_ws2"), ("ws3", "L0_ffn_ws3")]:
        m[nm] = np.ascontiguousarray(np.asarray(inp[key], dtype=np.float32))
    qkv_b = np.asarray(inp["L0_attn_qkv_b"], dtype=np.float32)
    m["qkbT"] = np.ascontiguousarray(qkv_b[0:2048].reshape(16, 128).T)
    m["rqbT"] = np.ascontiguousarray(
        np.asarray(inp["L0_attn_rq_b"], np.float32).reshape(2, 128).T)
    m["rkvbT"] = np.ascontiguousarray(
        np.asarray(inp["L0_attn_rkv_b"], np.float32).reshape(4, 128).T)
    be1 = np.asarray(inp["L0_ffn_be1"], np.float32)
    be3 = np.asarray(inp["L0_ffn_be3"], np.float32)
    m["be1T"] = np.ascontiguousarray(be1.reshape(16, 4, 128).transpose(2, 1, 0)
                                     .reshape(128, 64))
    m["be3T"] = np.ascontiguousarray(be3.reshape(16, 4, 128).transpose(2, 1, 0)
                                     .reshape(128, 64))
    bs1 = np.asarray(inp["L0_ffn_bs1"], np.float32)
    bs3 = np.asarray(inp["L0_ffn_bs3"], np.float32)
    m["bs1T"] = np.ascontiguousarray(bs1.reshape(8, 128).T)
    m["bs3T"] = np.ascontiguousarray(bs3.reshape(8, 128).T)
    m["be2"] = np.ascontiguousarray(np.asarray(inp["L0_ffn_be2"], np.float32))
    m["gbB"] = np.ascontiguousarray(
        np.tile(np.asarray(inp["L0_ffn_gate_b"], np.float32)[None, :], (128, 1)))
    gl = np.asarray(inp["L0_attn_gl_w"], np.float32)[:, 0]
    gr = np.asarray(inp["L0_attn_gr_w"], np.float32)[:, 0]
    m["glwB"] = np.ascontiguousarray(np.tile(np.tile(gl, 16)[None, :], (128, 1)))
    m["grwB"] = np.ascontiguousarray(np.tile(np.tile(gr, 16)[None, :], (128, 1)))
    lat = np.zeros((NT, 64), np.float32)
    for i in range(NT):
        lat[i, q * 16 + i:] = NEG
    m["latmaskB"] = np.ascontiguousarray(np.tile(lat.reshape(1, -1), (128, 1)))
    m["identr"] = np.eye(128, dtype=np.float32)
    m["oneover"] = np.full((128, 1), 1.0 / 128, np.float32)
    m["epsc"] = np.full((128, 1), EPS, np.float32)
    ones = np.zeros((128, 128), np.float32)
    ones[[0, 32, 64], :] = 1.0
    m["onespage"] = ones
    bp = np.zeros((128, 2048), np.float32)
    bp[0, 0:1024] = qkv_b[2048:3072]
    bp[0, 1024:2048] = np.asarray(inp["L0_ffn_bs2"], np.float32)
    m["biaspage"] = bp
    ob = np.zeros((128, 1024), np.float32)
    ob[0, :] = np.asarray(inp["L0_attn_out_b"], np.float32)
    m["outbB"] = ob
    rb2 = np.zeros((128, 1024), np.float32)
    rb2[0, :] = np.asarray(inp["L0_attn_rout_b"], np.float32)
    m["routbB"] = rb2
    mA = np.zeros((128, 256), np.float32)
    qq = np.arange(128)[:, None]
    kk = np.arange(128)[None, :]
    mA[:, 0:128] = np.where(kk >= qq, 0.0, NEG)
    mA[:, 128:256] = np.where(kk < qq, 0.0, NEG)
    mA0 = mA.copy()
    if q == 0:
        mA0[:, 0:128] = NEG

    def maskT(msk):
        b0 = np.ascontiguousarray(msk[:, 0:128].T)
        b1 = np.ascontiguousarray(msk[:, 128:256].T)
        return np.concatenate([b0, b0, b1, b1], axis=1).astype(np.float32)

    m["maskA"] = maskT(mA)
    m["maskA0"] = maskT(mA0)[:, 0:256]
    rm = np.ones((128, 1), np.float32)
    ra = np.zeros((128, 1), np.float32)
    if q == 0:
        rm[0, 0] = 0.0
        ra[0, 0] = 1.0
    m["rowmask0"] = rm
    m["row0add"] = ra
    m["latrow0"] = (np.zeros if q == 0 else np.ones)((128, 1)).astype(np.float32)
    return m


def kernel(**inputs):
    key = "nc"
    if key not in _CACHE:
        _CACHE[key] = build_program()
    nc, taps = _CACHE[key]
    in_maps = [_host_inputs(c, inputs) for c in range(8)]
    res = run_bass_kernel_spmd(nc, in_maps, core_ids=list(range(8)))
    out = np.empty((B, S, D), np.float32)
    for c in range(8):
        b, q = c // 4, c % 4
        out[b, q * R:(q + 1) * R] = res.results[c]["yout"]
    if taps:
        kernel.tap_results = [{t: res.results[c].get("tap_" + t) for t in taps}
                              for c in range(8)]
    return out


if __name__ == "__main__":
    import importlib.util
    spec = importlib.util.spec_from_file_location("_ref", "/root/problem/reference.py")
    ref = importlib.util.module_from_spec(spec)
    spec.loader.exec_module(ref)
    ins = {k: np.asarray(v) for k, v in ref.setup_inputs().items()}
    got = kernel(**ins)
    import jax.numpy as jnp
    want = np.asarray(ref.reference(**{k: jnp.asarray(v) for k, v in ins.items()}))
    err = np.linalg.norm(got - want) / np.linalg.norm(want)
    print("rel_err", err)


# version 22
# speedup vs baseline: 1.1511x; 1.0175x over previous
import os
import numpy as np
import concourse.bass as bass
import concourse.mybir as mybir
from concourse.bass_utils import run_bass_kernel_spmd

F32 = mybir.dt.float32
FR = mybir.dt.float32r
AF = mybir.ActivationFunctionType
ALU = mybir.AluOpType
AX = mybir.AxisListType

MM_EXACT = os.environ.get("MM_EXACT", "0") == "1"
MMDT = F32 if MM_EXACT else FR

B, S, D = 2, 8192, 1024
R, NT, STW = 2048, 16, 2
NST = NT // STW
E = 16
NEG = -1e9
EPS = 1e-5
RING_N = 9
RING_X = 3
TAP_NAMES = [t for t in os.environ.get("TAPS", "").split(",") if t]
KPH = int(os.environ.get("KPHASES", "3"))
KATT = int(os.environ.get("KATT", "99"))
KSUB = int(os.environ.get("KSUB", "4"))
KLAT = int(os.environ.get("KLAT", "9"))


class Sem:
    __slots__ = ("name", "step", "n", "h")

    def __init__(self, name, step=1):
        self.name, self.step, self.n, self.h = name, step, 0, None


class Prog:
    def __init__(self):
        self.events = {e: [] for e in ("tensor", "vector", "scalar", "gpsimd", "sync")}
        self.sems = []
        self.self_sem = {e: self.sem("self_" + e) for e in ("vector", "scalar", "gpsimd")}

    def sem(self, name, step=1):
        s = Sem(name, step)
        self.sems.append(s)
        return s

    def emit(self, eng, build, waits=(), inc=(), selfser=True):
        waits = [w for w in waits if w is not None]
        incs = [inc] if isinstance(inc, Sem) else list(inc)
        if eng in self.self_sem and selfser:
            ss = self.self_sem[eng]
            if ss.n:
                waits.append((ss, ss.n))
            incs.append(ss)
        toks = []
        for s in incs:
            s.n += s.step
            toks.append((s, s.n))
        self.events[eng].append((waits, build, incs))
        return toks

    def bar(self, eng):
        ss = self.self_sem[eng]
        return (ss, ss.n) if ss.n else None


def build_program():
    nc = bass.Bass("TRN2", target_bir_lowering=False, detect_race_conditions=False)
    P = Prog()
    taps = []

    def din(name, shape, dt=MMDT):
        return nc.dram_tensor(name, list(shape), dt, kind="ExternalInput").ap()

    d = {}
    d["xfull"] = din("xfull", [S, D], F32)
    d["xown"] = din("xown", [R, D], F32)
    d["xhalo"] = din("xhalo", [128, D], F32)
    for nm, sh in [("qkv_w", [D, 3 * D]), ("rq_w", [D, 256]), ("rkv_w", [D, 512]),
                   ("rout_w", [256, D]), ("out_w", [D, D]), ("gate_w", [16, D]),
                   ("We1", [E, D, 512]), ("We2", [E, 512, D]), ("We3", [E, D, 512]),
                   ("be2", [16, D]),
                   ("ws1", [D, D]), ("ws2", [D, D]), ("ws3", [D, D]),
                   ("identr", [128, 128]), ("maskA", [128, 512]), ("maskA0", [128, 256])]:
        d[nm] = din(nm, sh)
    for nm, sh in [("qkbT", [128, 16]), ("rqbT", [128, 2]), ("rkvbT", [128, 4]),
                   ("be1T", [128, 64]), ("be3T", [128, 64]), ("bs1T", [128, 8]),
                   ("bs3T", [128, 8]), ("gbB", [128, 16]),
                   ("glwB", [128, D]), ("grwB", [128, D]), ("latmaskB", [128, NT * 64]),
                   ("oneover", [128, 1]), ("epsc", [128, 1]), ("onespage", [128, 128]),
                   ("biaspage", [128, 2048]), ("outbB", [128, 1024]), ("routbB", [128, 1024]), ("rowmask0", [128, 1]),
                   ("row0add", [128, 1]), ("latrow0", [128, 1])]:
        d[nm] = din(nm, sh, F32)
    yout = nc.dram_tensor("yout", [R, D], F32, kind="ExternalOutput").ap()

    A_COLS = 15040
    RO_A = RING_N * 512
    RO_rkT = RO_A + A_COLS
    RO_rv = RO_rkT + 256
    RO_routw = RO_rv + 256
    RO_gwT = RO_routw + 2048
    RO_attnsb = RO_gwT + 128
    RO_attnT = RO_attnsb + 1024
    RO_rwg = RO_attnT
    R_COLS = RO_rwg + 32
    AO_lnT, AO_ln, AO_qT, AO_kT, AO_v, AO_rqT = 0, 2048, 3072, 5120, 11264, 14528
    AO_latexp, AO_latT, AO_latoutT, AO_mixedT = 0, 256, 768, 1024
    AO_ln2T, AO_h = 0, 8192
    AO_pooledT, AO_rkvT = 0, 512

    B_COLS = 8192
    BO_xring, BO_lnpool = 0, 2048
    BO_local, BO_remote, BO_tmp = 0, 1024, 2048
    BO_glwB, BO_grwB, BO_latmaskB = 4096, 5120, 6144
    BO_outb = 7168
    FO_xres = 0
    FO_B = NT * 1024
    misc = {}
    _off = [FO_B + B_COLS]

    def malloc(name, cols):
        misc[name] = (_off[0], cols)
        _off[0] += cols

    for nm, cc in [("qkbT", 16), ("rqbT", 2), ("rkvbT", 4), ("be1T", 64), ("be3T", 64),
                   ("bs1T", 8), ("bs3T", 8), ("gbB", 16), ("oneover", 1), ("epsc", 1),
                   ("rowmask0", 1), ("row0add", 1), ("latrow0", 1), ("onespage", 128),
                   ("biaspage", 2048), ("combT", 128), ("zerop", 16), ("routbg", 16),
                   ("latsc", 256), ("routbB", 1024)]:
        malloc(nm, cc)
    for p in range(2):
        for nm, cc in [("stat6", 12), ("stat2", 2), ("std", 1), ("rstd", 1), ("nmr", 1),
                       ("negmax2", 2), ("bias2", 2), ("sums", 16), ("recips", 16),
                       ("negmaxL", 4), ("biasL", 4), ("latsums", 4),
                       ("latrecip", 4), ("probs", 16), ("s2", 16), ("scr16a", 16),
                       ("scr16b", 16), ("scr4a", 4), ("scr4b", 4), ("scr4c", 4),
                       ("scr1a", 1), ("scr1b", 1), ("lg", 16), ("rg", 16), ("gate", 16)]:
            malloc(f"{nm}{p}", cc)
    for p in range(2, 4):
        for nm, cc in [("stat6", 12), ("stat2", 2), ("std", 1), ("rstd", 1),
                       ("nmr", 1)]:
            malloc(f"{nm}{p}", cc)
    F_COLS = _off[0]
    assert (R_COLS + F_COLS + 128 + 768) * 4 <= 212700, \
        f"SBUF over: {(R_COLS + F_COLS + 128 + 768) * 4}"

    Rb = nc.alloc_sbuf_tensor("Rar", [128, R_COLS], MMDT).ap()
    Fb = nc.alloc_sbuf_tensor("Far", [128, F_COLS], F32).ap()
    ident_t = nc.alloc_sbuf_tensor("identsb", [128, 128], MMDT).ap()
    maskA_t = nc.alloc_sbuf_tensor("maskAsb", [128, 512], MMDT).ap()
    maskA0_t = nc.alloc_sbuf_tensor("maskA0sb", [128, 256], MMDT).ap()
    IDENT = ident_t[:, :]

    def rv_(off, cols, rows=128, r0=0):
        return Rb[r0:r0 + rows, off:off + cols]

    def fv_(off, cols, rows=128, r0=0):
        return Fb[r0:r0 + rows, off:off + cols]

    def mv(name, p=None):
        off, cols = misc[name if p is None else f"{name}{p}"]
        return Fb[0:128, off:off + cols]

    pbk = [nc.alloc_psum_tensor(f"pb{i}", [128, 512], F32).ap() for i in range(6)]
    tbk = [nc.alloc_psum_tensor(f"tb{i}", [128, 512], MMDT).ap() for i in range(2)]

    lnT = rv_(RO_A + AO_lnT, 2048).rearrange("p (c t) -> p c t", c=8)
    ln_t = rv_(RO_A + AO_ln, 1024)
    qT = rv_(RO_A + AO_qT, 2048).rearrange("p (c t) -> p c t", c=8)
    kT = rv_(RO_A + AO_kT, 6144).rearrange("p (c t) -> p c t", c=16)
    vV65 = rv_(RO_A + AO_v, 3264).rearrange("p (s h e) -> p s h e", s=3, h=16)
    vVf = rv_(RO_A + AO_v, 3264).rearrange("p (s e) -> p s e", s=3)
    rqT = rv_(RO_A + AO_rqT, 512).rearrange("p (c t) -> p c t", c=2)
    kTz_tok = [None]
    mixedT = rv_(RO_A + AO_mixedT, 1024)
    ln2T = rv_(RO_A + AO_ln2T, 8192).rearrange("p (c t) -> p c t", c=8)
    hV = rv_(RO_A + AO_h, 4096).rearrange("p (g c t) -> p g c t", g=2, c=4)
    ln2_t = rv_(RO_attnsb, 1024)
    pooledT = rv_(RO_A + AO_pooledT, 512).rearrange("p (c l) -> p c l", c=8)
    routw = rv_(RO_routw, 2048).rearrange("p (c e) -> p c e", c=2)
    gwT = rv_(RO_gwT, 128).rearrange("p (c e) -> p c e", c=8)
    attnT_sb = [rv_(RO_attnsb, 512), rv_(RO_attnsb + 512, 512)]
    xres = fv_(FO_xres, NT * 1024).rearrange("p (t e) -> p t e", t=NT)
    local_v = fv_(FO_B + BO_local, 1024)
    remote_v = fv_(FO_B + BO_remote, 1024)
    mixed_v = rv_(RO_A + AO_ln, 1024)
    glwB = fv_(FO_B + BO_glwB, 1024)
    grwB = fv_(FO_B + BO_grwB, 1024)
    latmaskB = fv_(FO_B + BO_latmaskB, 1024)
    y_sb = fv_(FO_B, 8192).rearrange("p (t e) -> p t e", t=8)
    xring = [fv_(FO_B + BO_xring + i * 1024, 1024) for i in range(2)]
    lnpool = [fv_(FO_B + BO_lnpool, 1024), fv_(FO_B + BO_lnpool, 1024)]
    ONES = mv("onespage")
    BIASP = mv("biaspage")

    t_ev = P.sem("t_ev")
    g_ld = P.sem("g_ld", 16)
    so = P.sem("so", 16)
    rg_w = [P.sem(f"rw{i}", 16) for i in range(RING_N + RING_X)]
    xr_w = [P.sem(f"xw{i}", 16) for i in range(2)]

    bank_free, bank_open = {}, {}

    def bank_wait(b):
        key = id(b.tensor)
        assert not bank_open.get(key, False), "psum bank reopened while open"
        bank_open[key] = True
        t = bank_free.get(key)
        return [t] if t else []

    def bank_close(b, tok):
        key = id(b.tensor)
        bank_open[key] = False
        bank_free[key] = tok

    SLOT_OFF = [s * 512 for s in range(RING_N)] + \
               [RO_A + 13312 + 512 * k for k in range(RING_X)]
    rst = {"next": 0, "loads": [0] * (RING_N + RING_X),
           "free": [None] * (RING_N + RING_X),
           "lhist": [[] for _ in range(RING_N + RING_X)],
           "active": RING_N}

    def ring_load(ap_fn, cols=512):
        s = rst["next"]
        rst["next"] = (s + 1) % rst["active"]
        j = rst["loads"][s]
        rst["loads"][s] += 1
        rst["lhist"][s].append(rst.get("ph"))
        assert j == 0 or rst["free"][s] is not None, \
            f"slot {s} reloaded unreleased (ph={rst.get('ph')}, load #{j}); " + \
            f"history={rst['lhist'][s]}"
        w = [rst["free"][s]] if rst["free"][s] else []
        rst["free"][s] = None
        P.emit("sync", (lambda s=s, f=ap_fn, c=cols:
                        nc.sync.dma_start(out=rv_(SLOT_OFF[s], c), in_=f())),
               waits=w, inc=[rg_w[s]], selfser=False)
        return s, (rg_w[s], 16 * rst["loads"][s])

    def ring_release(slots, tok):
        for sl in slots:
            s = sl[0] if isinstance(sl, tuple) else sl
            assert rst["free"][s] is None, f"slot {s} double release"
            rst["lhist"][s].append("REL:" + str(rst.get("ph")))
            rst["free"][s] = tok

    def slot(s, cols=512):
        return rv_(SLOT_OFF[s], cols)

    def T(build, waits=(), tok=False):
        r = P.emit("tensor", build, waits=waits, inc=([t_ev] if tok else []))
        return r[0] if tok else None

    def V(build, waits=()):
        return P.emit("vector", build, waits=waits)[-1]

    def SC(build, waits=()):
        return P.emit("scalar", build, waits=waits)[-1]

    def G(build, waits=()):
        return P.emit("gpsimd", build, waits=waits)[-1]

    def GD(build, waits=()):
        return P.emit("gpsimd", build, waits=waits, inc=[g_ld], selfser=False)[0]

    def tap(name, ap_fn, shape, waits):
        if name not in TAP_NAMES:
            return None
        t = nc.dram_tensor("tap_" + name, list(shape), F32, kind="ExternalOutput").ap()
        taps.append(name)
        return GD(lambda t=t, f=ap_fn: nc.gpsimd.dma_start(out=t, in_=f()),
                  waits=waits)

    ln_guard = {0: None, 1: None, 2: None, 3: None}
    ln_par = [0]

    def ln_stats(src0_fn, src1_fn, waits=()):
        p = ln_par[0] = (ln_par[0] + 1) % 4
        w = list(waits) + ([ln_guard[p]] if ln_guard[p] else [])
        V(lambda p=p, f=src0_fn: nc.vector.bn_stats(mv("stat6", p)[:, 0:6], f()), waits=w)
        V(lambda p=p, f=src1_fn: nc.vector.bn_stats(mv("stat6", p)[:, 6:12], f()))
        v3 = V(lambda p=p: nc.vector.bn_aggr(mv("stat2", p), mv("stat6", p)))
        s1 = SC(lambda p=p: nc.scalar.activation(mv("std", p), mv("stat2", p)[:, 1:2],
                                                 AF.Sqrt, bias=mv("epsc")[:, 0:1],
                                                 scale=1.0), waits=[v3])
        V(lambda p=p: nc.vector.reciprocal(mv("rstd", p), mv("std", p)), waits=[s1])
        nm = V(lambda p=p: nc.vector.tensor_scalar(mv("nmr", p), mv("stat2", p)[:, 0:1],
                                                   mv("rstd", p)[:, 0:1], -1.0,
                                                   ALU.mult, ALU.mult))
        return p, nm

    def transposes8(src_fn, dst_fn, waits):
        last_tt, last_cp = None, None
        for rnd in range(2):
            b = tbk[rnd]
            w = list(waits) + bank_wait(b)
            for c in range(4):
                last_tt = T(lambda b=b, c=c, s=src_fn, rnd=rnd:
                            nc.tensor.transpose(b[:, c * 128:(c + 1) * 128],
                                                s(rnd * 4 + c), IDENT),
                            waits=(w if c == 0 else []), tok=(c == 3))
            last_cp = V(lambda b=b, rnd=rnd, df=dst_fn: nc.vector.tensor_copy(
                df(rnd), b[:, 0:512].rearrange("p (c t) -> p c t", c=4)),
                waits=[last_tt])
            bank_close(b, last_cp)
        return last_tt, last_cp

    ncl = [0]

    def cload(dst_ap, src_name):
        GD(lambda a=dst_ap, s=src_name: nc.gpsimd.dma_start(out=a, in_=d[s]))
        ncl[0] += 1

    for nm in ["qkbT", "rqbT", "rkvbT", "be1T", "be3T", "bs1T", "bs3T", "gbB",
               "oneover", "epsc", "rowmask0", "row0add", "latrow0", "onespage",
               "biaspage"]:
        cload(mv(nm), nm)
    cload(fv_(FO_B + BO_outb, 1024), "outbB")
    cload(mv("routbB"), "routbB")
    cload(glwB, "glwB")
    cload(grwB, "grwB")
    cload(latmaskB, "latmaskB")
    cload(IDENT, "identr")
    cload(maskA_t[:, :], "maskA")
    cload(maskA0_t[:, :], "maskA0")
    GD(lambda: nc.gpsimd.dma_start(
        out=rv_(RO_routw, 2048).rearrange("p (c e) -> p c e", c=2),
        in_=d["rout_w"].rearrange("(c p) e -> p c e", p=128)))
    ncl[0] += 1
    gwraw = Rb[0:16, RO_attnsb:RO_attnsb + 1024]
    cload(gwraw, "gate_w")
    GD(lambda: nc.gpsimd.dma_start(out=xres[:, :, :],
                                   in_=d["xown"].rearrange("(t p) e -> p t e", p=128)))
    ncl[0] += 1
    xhalo_sb = lnpool[0]
    GD(lambda: nc.gpsimd.dma_start(out=xhalo_sb, in_=d["xhalo"]))
    ncl[0] += 1
    consts_tok = (g_ld, 16 * ncl[0])

    V(lambda: nc.vector.memset(mv("zerop"), 0.0), waits=[consts_tok])

    zk = SC(lambda: nc.scalar.activation(rv_(RO_A + AO_kT, 6144),
                                         rv_(RO_A + AO_kT, 6144), AF.Identity,
                                         bias=0.0, scale=0.0), waits=[consts_tok])
    zr = SC(lambda: nc.scalar.activation(rv_(RO_rkT, 256), rv_(RO_rkT, 256),
                                         AF.Identity, bias=0.0, scale=0.0))
    kTz_tok[0] = zr
    SC(lambda: nc.scalar.activation(
        rv_(RO_A + AO_v, 3264).rearrange("p (x e) -> p x e", e=68)[:, :, 64:68],
        rv_(RO_A + AO_v, 3264).rearrange("p (x e) -> p x e", e=68)[:, :, 64:68],
        AF.Identity, bias=1.0, scale=0.0))

    rwg = rv_(RO_rwg, 32)
    for c2 in range(2):
        for hf in range(2):
            b0 = tbk[0].bitcast(F32)
            bw0 = bank_wait(tbk[0])
            V(lambda c2=c2, hf=hf, b0=b0: nc.vector.tensor_tensor(
                b0[:, 0:512], routw[:, c2, hf * 512:(hf + 1) * 512].bitcast(F32),
                grwB[:, hf * 512:(hf + 1) * 512], ALU.mult),
                waits=bw0 + [consts_tok])
            def _rwg(c2=c2, hf=hf, b0=b0):
                with nc.allow_low_precision(reason="f32r bits are fp32"):
                    return nc.vector.reduce_sum(
                        rwg[:, c2 * 16 + hf * 8:c2 * 16 + hf * 8 + 8],
                        b0[:, 0:512].rearrange("p (h e) -> p h e", h=8), axis=AX.X)
            rt = V(_rwg)
            bank_close(tbk[0], rt)
    for hf in range(2):
        b0 = tbk[0].bitcast(F32)
        bw0 = bank_wait(tbk[0])
        V(lambda hf=hf, b0=b0: nc.vector.tensor_tensor(
            b0[0:1, 0:512], mv("routbB")[0:1, hf * 512:(hf + 1) * 512],
            grwB[0:1, hf * 512:(hf + 1) * 512], ALU.mult), waits=bw0)
        rt = V(lambda hf=hf, b0=b0: nc.vector.reduce_sum(
            mv("routbg")[0:1, hf * 8:hf * 8 + 8],
            b0[0:1, 0:512].rearrange("p (h e) -> p h e", h=8), axis=AX.X))
        bank_close(tbk[0], rt)

    gwT_tok = None
    for rnd in range(2):
        b = tbk[rnd]
        w = [consts_tok] + bank_wait(b)
        tt = None
        for c in range(4):
            cc = rnd * 4 + c
            tt = T(lambda b=b, c=c, cc=cc: nc.tensor.transpose(
                b[:, c * 16:(c + 1) * 16], gwraw[0:16, cc * 128:(cc + 1) * 128],
                IDENT[0:16, 0:16]), waits=(w if c == 0 else []), tok=(c == 3))
        cp = V(lambda b=b, rnd=rnd: nc.vector.tensor_copy(
            rv_(RO_gwT + rnd * 64, 64), b[:, 0:64]), waits=[tt])
        bank_close(b, cp)
        gwT_tok = cp

    p, nmtok = ln_stats(lambda: xhalo_sb[:, 0:512], lambda: xhalo_sb[:, 512:1024],
                        [consts_tok])
    lnh = SC(lambda p=p: nc.scalar.activation(ln_t, xhalo_sb, AF.Identity,
                                              bias=mv("nmr", p)[:, 0:1],
                                              scale=mv("rstd", p)[:, 0:1]),
             waits=[nmtok])
    ln_guard[p] = lnh
    htt, hcp = transposes8(lambda c: rv_(RO_A + AO_ln + c * 128, 128),
                           lambda rnd: lnT[:, rnd * 4:(rnd + 1) * 4, 0:128], [lnh])
    for mg in (2, 3):
        slots = [ring_load(lambda k=k, mg=mg:
                           d["qkv_w"][k * 128:(k + 1) * 128, mg * 512:(mg + 1) * 512])
                 for k in range(8)]
        for m in range(4):
            ch = (mg - 2) * 4 + m
            bk = pbk[m % 2]
            bw = bank_wait(bk) + [hcp]
            mm = None
            for k in range(8):
                s_, stk = slots[k]
                mm = T(lambda bk=bk, s_=s_, m=m, k=k: nc.tensor.matmul(
                    bk[:, 0:128], slot(s_)[:, m * 128:(m + 1) * 128], lnT[:, k, 0:128],
                    start=(k == 0), stop=(k == 7)),
                    waits=(bw if k == 0 else []) + [stk], tok=(k == 7))
            SC(lambda bk=bk, ch=ch: nc.scalar.activation(
                kT[0:64, 2 * ch, 0:128], bk[0:64, 0:128], AF.Identity,
                bias=mv("qkbT")[0:64, 8 + ch:9 + ch], scale=1.0),
               waits=[mm, kTz_tok[0]])
            ev = SC(lambda bk=bk, ch=ch: nc.scalar.activation(
                kT[64:128, 2 * ch + 1, 0:128], bk[64:128, 0:128], AF.Identity,
                bias=mv("qkbT")[64:128, 8 + ch:9 + ch], scale=1.0))
            bank_close(bk, ev)
        ring_release(slots, mm)
    rst['ph'] = 'halo_v'
    for mg in (4, 5):
        slots = [ring_load(lambda k=k, mg=mg:
                           d["qkv_w"][k * 128:(k + 1) * 128, mg * 512:(mg + 1) * 512])
                 for k in range(8)]
        bk = pbk[2 + (mg - 4)]
        bw = bank_wait(bk) + [hcp]
        for k in range(8):
            s_, stk = slots[k]
            T(lambda bk=bk, s_=s_, k=k: nc.tensor.matmul(
                bk[:, 0:512], lnT[:, k, 0:128], slot(s_)[:, 0:512],
                start=(k == 0), stop=False),
                waits=(bw if k == 0 else []) + [stk])
        mm = T(lambda bk=bk, mg=mg: nc.tensor.matmul(
            bk[:, 0:512], ONES[0:1, 0:128], BIASP[0:1, (mg - 4) * 512:(mg - 3) * 512],
            start=False, stop=True), tok=True)
        ev = SC(lambda bk=bk, mg=mg: nc.scalar.copy(
            vV65[:, 0, (mg - 4) * 8:(mg - 3) * 8, 0:64],
            bk[:, 0:512].rearrange("p (h e) -> p h e", h=8)), waits=[mm])
        b0 = tbk[0].bitcast(F32)
        bw0 = bank_wait(tbk[0])
        vgm = V(lambda bk=bk, mg=mg, b0=b0: nc.vector.tensor_tensor(
            b0[:, 0:512], bk[:, 0:512], glwB[:, (mg - 4) * 512:(mg - 3) * 512],
            ALU.mult), waits=bw0 + [mm, ev])
        bank_close(bk, vgm)
        def _vg0(mg=mg, b0=b0):
            with nc.allow_low_precision(reason="f32r bits are fp32"):
                return nc.vector.reduce_sum(
                    vV65[:, 0, (mg - 4) * 8:(mg - 3) * 8, 65:66],
                    b0[:, 0:512].rearrange("p (h e) -> p h e", h=8), axis=AX.X)
        vgr = V(_vg0)
        bank_close(tbk[0], vgr)
        ring_release(slots, mm)

    pool_bank = pbk[5]
    pool_w = bank_wait(pool_bank)
    xr_n = [0, 0]
    xr_free = {0: None, 1: None}
    last_pool = {0: None, 1: None}
    pool_mm = None
    for cch in range(64):
        sl = cch % 2
        j = xr_n[sl]
        xr_n[sl] += 1
        w = [xr_free[sl]] if xr_free[sl] else []
        P.emit("sync", (lambda sl=sl, cch=cch: nc.sync.dma_start(
            out=xring[sl], in_=d["xfull"][cch * 128:(cch + 1) * 128, :])),
            waits=w, inc=[xr_w[sl]], selfser=False)
        xtok = (xr_w[sl], 16 * xr_n[sl])
        p, nmtok = ln_stats(lambda sl=sl: xring[sl][:, 0:512],
                            lambda sl=sl: xring[sl][:, 512:1024], [xtok])
        lp = 0
        ap = SC(lambda p=p, sl=sl, lp=lp: nc.scalar.activation(
            lnpool[lp], xring[sl], AF.Identity, bias=mv("nmr", p)[:, 0:1],
            scale=mv("rstd", p)[:, 0:1]),
            waits=[nmtok, last_pool[lp]])
        ln_guard[p] = ap
        xr_free[sl] = ap
        for fc in range(8):
            pool_mm = T(lambda fc=fc, cch=cch, lp=lp: nc.tensor.matmul(
                pool_bank[:, fc * 64 + cch:fc * 64 + cch + 1],
                lnpool[lp][:, fc * 128:(fc + 1) * 128], mv("oneover")[:, 0:1],
                start=(cch == 0 and fc == 0), stop=(cch == 63 and fc == 7)),
                waits=(([ap] + pool_w) if fc == 0 else []),
                tok=(fc == 7))
        last_pool[lp] = pool_mm
    pool_ev = SC(lambda: nc.scalar.copy(rv_(RO_A + AO_pooledT, 512),
                                        pool_bank[:, 0:512]), waits=[pool_mm])
    bank_close(pool_bank, pool_ev)
    tap("pooledT", lambda: rv_(RO_A + AO_pooledT, 512).bitcast(F32), [128, 512],
        [pool_ev])

    rst['ph'] = 'rkv'
    slots = [ring_load(lambda k=k: d["rkv_w"][k * 128:(k + 1) * 128, :])
             for k in range(8)]
    rkv_toks = []
    for m in range(4):
        bk = pbk[m % 2]
        bw = bank_wait(bk) + [pool_ev]
        mm = None
        for k in range(8):
            s_, stk = slots[k]
            mm = T(lambda bk=bk, s_=s_, m=m, k=k: nc.tensor.matmul(
                bk[:, 0:64], slot(s_)[:, m * 128:(m + 1) * 128], pooledT[:, k, :],
                start=(k == 0), stop=(k == 7)),
                waits=(bw if k == 0 else []) + [stk], tok=(k == 7))
        ev = SC(lambda bk=bk, m=m: nc.scalar.activation(
            rv_(RO_A + AO_rkvT + m * 64, 64), bk[:, 0:64], AF.Identity,
            bias=mv("rkvbT")[:, m:m + 1], scale=1.0), waits=[mm])
        bank_close(bk, ev)
        rkv_toks.append(ev)
        if m == 3:
            ring_release(slots, mm)
    for lh in range(4):
        r0 = (lh % 2) * 64
        mch = lh // 2
        rk_tok = V(lambda lh=lh, r0=r0, mch=mch: nc.vector.tensor_copy(
            Rb[r0:r0 + 64, RO_rkT + lh * 64:RO_rkT + (lh + 1) * 64],
            Rb[r0:r0 + 64, RO_A + AO_rkvT + mch * 64:RO_A + AO_rkvT + (mch + 1) * 64]),
            waits=rkv_toks[0:2] + [kTz_tok[0]] if lh == 0 else [])
    bt = tbk[0]
    bw = bank_wait(bt) + rkv_toks[2:4]
    tt = None
    for m in (2, 3):
        tt = T(lambda bt=bt, m=m: nc.tensor.transpose(
            bt[0:64, (m - 2) * 128:(m - 1) * 128], rv_(RO_A + AO_rkvT + m * 64, 64),
            IDENT), waits=(bw if m == 2 else []), tok=(m == 3))
    rvv_tok = V(lambda bt=bt: nc.vector.tensor_copy(Rb[0:64, RO_rv:RO_rv + 256],
                                                    bt[0:64, 0:256]), waits=[tt])
    bank_close(bt, rvv_tok)
    tap("rkT", lambda: rv_(RO_rkT, 128).bitcast(F32), [128, 128], [rk_tok])
    tap("rvv", lambda: Rb[0:64, RO_rv:RO_rv + 256].bitcast(F32), [64, 256], [rvv_tok])

    attn_guard = [None, None]
    if KPH < 2:
        for ti in range(NT):
            P.emit("gpsimd", (lambda ti=ti: nc.gpsimd.dma_start(
                out=yout[ti * 128:(ti + 1) * 128, :], in_=xres[:, ti, :])),
                waits=[consts_tok, rvv_tok], inc=[so], selfser=False)
        P.emit("gpsimd", lambda: nc.gpsimd.engine_nop(), waits=[(so, 16 * NT)],
               selfser=False)
        for s in P.sems:
            s.h = nc.alloc_semaphore(s.name)
        with nc.Block() as block:
            def mk0(eng):
                def run(proxy):
                    hw = {}
                    for waits, build, incs in P.events[eng]:
                        for sm, v in waits:
                            if hw.get(id(sm), 0) < v:
                                proxy.wait_ge(sm.h, v)
                                hw[id(sm)] = v
                        ins = build()
                        for sm in incs:
                            ins.then_inc(sm.h, sm.step)
                return run
            block.tensor(mk0("tensor"))
            block.vector(mk0("vector"))
            block.scalar(mk0("scalar"))
            block.gpsimd(mk0("gpsimd"))
            block.sync(mk0("sync"))
        return nc, taps

    attnT_guard = [None, None]
    lat_scratch_guard = [None]
    mixed_guard = None
    halo_guard = None
    lat_guard = None
    lnt_guard = None
    xres_tok = [None] * NT
    lnT_box = [None]
    lnT_free = [None]

    ln_pages = {}

    def emit_LNstats(st2):
        for j in range(STW):
            ti = st2 * STW + j
            ln_pages[ti] = ln_stats(lambda ti=ti: xres[:, ti, 0:512],
                                    lambda ti=ti: xres[:, ti, 512:1024],
                                    [xres_tok[ti]])

    def emit_LNapply(st2):
        nonlocal lnt_guard
        lnT_cp = None
        for j in range(STW):
            ti = st2 * STW + j
            p, nmtok = ln_pages[ti]
            ap = SC(lambda p=p, ti=ti: nc.scalar.activation(
                ln_t, xres[:, ti, :], AF.Identity, bias=mv("nmr", p)[:, 0:1],
                scale=mv("rstd", p)[:, 0:1]), waits=[nmtok, lnt_guard, mixed_guard])
            ln_guard[p] = ap
            lnt_guard, lnT_cp = transposes8(
                lambda c: rv_(RO_A + AO_ln + c * 128, 128),
                lambda rnd, j=j: lnT[:, rnd * 4:(rnd + 1) * 4, j * 128:(j + 1) * 128],
                [ap] + ([lnT_free[0]] if (j == 0 and lnT_free[0]) else []))
        if st2 == 0:
            tap("lnT0", lambda: rv_(RO_A + AO_lnT, 2048).bitcast(F32), [128, 2048],
                [lnT_cp])
        lnT_box[0] = lnT_cp

    for st in range(NST):
        if st == 0:
            emit_LNstats(0)
        emit_LNapply(st)
        lnT_cp = lnT_box[0]
        rst['ph'] = f'proj{st}'
        for mg in range(6):
            slots = [ring_load(lambda k=k, mg=mg:
                               d["qkv_w"][k * 128:(k + 1) * 128, mg * 512:(mg + 1) * 512])
                     for k in range(8)]
            if mg < 4:
                for m in range(4):
                    ch = mg * 4 + m
                    bk = pbk[m % 2]
                    bw = bank_wait(bk) + [lnT_cp]
                    mm = None
                    for k in range(8):
                        s_, stk = slots[k]
                        mm = T(lambda bk=bk, s_=s_, m=m, k=k: nc.tensor.matmul(
                            bk[:, 0:256], slot(s_)[:, m * 128:(m + 1) * 128],
                            lnT[:, k, :], start=(k == 0), stop=(k == 7)),
                            waits=(bw if k == 0 else []) + [stk], tok=(k == 7))
                    if mg < 2:
                        ev = SC(lambda bk=bk, ch=ch: nc.scalar.activation(
                            qT[:, ch, :], bk[:, 0:256], AF.Identity,
                            bias=mv("qkbT")[:, ch:ch + 1], scale=1.0), waits=[mm])
                    else:
                        SC(lambda bk=bk, ch=ch: nc.scalar.activation(
                            kT[0:64, 2 * (ch - 8), 128:384], bk[0:64, 0:256],
                            AF.Identity,
                            bias=mv("qkbT")[0:64, ch:ch + 1], scale=1.0),
                           waits=[mm, halo_guard])
                        ev = SC(lambda bk=bk, ch=ch: nc.scalar.activation(
                            kT[64:128, 2 * (ch - 8) + 1, 128:384], bk[64:128, 0:256],
                            AF.Identity,
                            bias=mv("qkbT")[64:128, ch:ch + 1], scale=1.0))
                    bank_close(bk, ev)
                    if m == 3:
                        ring_release(slots, mm)
            else:
                for j2 in range(STW):
                    bk = pbk[2 + j2]
                    bw = bank_wait(bk) + [lnT_cp, halo_guard]
                    for k in range(8):
                        s_, stk = slots[k]
                        T(lambda bk=bk, s_=s_, j2=j2, k=k: nc.tensor.matmul(
                            bk[:, 0:512], lnT[:, k, j2 * 128:(j2 + 1) * 128],
                            slot(s_)[:, 0:512], start=(k == 0), stop=False),
                            waits=(bw if k == 0 else []) + [stk])
                    mm = T(lambda bk=bk, mg=mg: nc.tensor.matmul(
                        bk[:, 0:512], ONES[0:1, 0:128],
                        BIASP[0:1, (mg - 4) * 512:(mg - 3) * 512],
                        start=False, stop=True), tok=True)
                    ev = SC(lambda bk=bk, j2=j2, mg=mg: nc.scalar.copy(
                        vV65[:, 1 + j2, (mg - 4) * 8:(mg - 3) * 8, 0:64],
                        bk[:, 0:512].rearrange("p (h e) -> p h e", h=8)),
                        waits=[mm])
                    b0 = tbk[0].bitcast(F32)
                    bw0 = bank_wait(tbk[0])
                    vgm = V(lambda bk=bk, mg=mg, b0=b0: nc.vector.tensor_tensor(
                        b0[:, 0:512], bk[:, 0:512],
                        glwB[:, (mg - 4) * 512:(mg - 3) * 512], ALU.mult),
                        waits=bw0 + [mm, ev])
                    bank_close(bk, vgm)
                    def _vgs(j2=j2, mg=mg, b0=b0):
                        with nc.allow_low_precision(reason="f32r bits are fp32"):
                            return nc.vector.reduce_sum(
                                vV65[:, 1 + j2, (mg - 4) * 8:(mg - 3) * 8, 65:66],
                                b0[:, 0:512].rearrange("p (h e) -> p h e", h=8),
                                axis=AX.X)
                    vgr = V(_vgs)
                    bank_close(tbk[0], vgr)
                    if j2 == STW - 1:
                        ring_release(slots, mm)
        slots = [ring_load(lambda k=k: d["rq_w"][k * 128:(k + 1) * 128, :], cols=256)
                 for k in range(8)]
        for m in range(2):
            bk = pbk[4 + m]
            bw = bank_wait(bk) + [lnT_cp]
            mm = None
            for k in range(8):
                s_, stk = slots[k]
                mm = T(lambda bk=bk, s_=s_, m=m, k=k: nc.tensor.matmul(
                    bk[:, 0:256], slot(s_, 256)[:, m * 128:(m + 1) * 128],
                    lnT[:, k, :], start=(k == 0), stop=(k == 7)),
                    waits=(bw if k == 0 else []) + [stk], tok=(k == 7))
            ev = SC(lambda bk=bk, m=m: nc.scalar.activation(
                rqT[:, m, :], bk[:, 0:256], AF.Identity,
                bias=mv("rqbT")[:, m:m + 1], scale=1.0), waits=[mm])
            bank_close(bk, ev)
            if m == 1:
                ring_release(slots, mm)
                lnT_free[0] = mm
        proj_done = P.bar("scalar")
        if st == 0:
            tap("qT0", lambda: rv_(RO_A + AO_qT, 2048).bitcast(F32), [128, 2048],
                [proj_done])
            tap("kT0", lambda: rv_(RO_A + AO_kT, 3072).bitcast(F32), [128, 3072],
                [proj_done])
            tap("v0", lambda: rv_(RO_A + AO_v, 3072).bitcast(F32), [128, 3072],
                [proj_done])

        lv_ = lambda ti: fv_(FO_B + (ti % 2) * 1024, 1024)
        rv2_ = lambda ti: fv_(FO_B + 2048 + (ti % 2) * 1024, 1024)
        tistate = {}
        def stP1(j):
            ti = st * STW + j
            pt = ti % 2
            maskap = maskA0_t if ti == 0 else maskA_t
            av_banks = [pbk[4], pbk[5], tbk[1].bitcast(F32)]
            av_w = [bank_wait(b) for b in av_banks]
            av_last = [None, None, None]
            for hg in range(8):
                hgp = hg % 2
                bk = pbk[hgp]
                bw = bank_wait(bk) + [proj_done, halo_guard]
                for c in range(2):
                    for hh in range(2):
                        h = hg * 2 + hh
                        T(lambda bk=bk, c=c, hh=hh, h=h, hg=hg, j=j:
                          nc.tensor.matmul(
                            bk[:, (c * 2 + hh) * 128:(c * 2 + hh + 1) * 128],
                            Rb[:, RO_A + AO_kT + h * 384 + (j + c) * 128:
                               RO_A + AO_kT + h * 384 + (j + c + 1) * 128],
                            Rb[:, RO_A + AO_qT + hg * 256 + j * 128:
                               RO_A + AO_qT + hg * 256 + (j + 1) * 128],
                            start=(c == 0 and hh == 0), stop=False),
                            waits=(bw if (c == 0 and hh == 0) else []))
                mm = None
                for c in range(2):
                    mk = (maskA0_t[:, 0:256] if (ti == 0 and c == 0)
                          else maskA_t[:, c * 256:(c + 1) * 256])
                    mm = T(lambda bk=bk, c=c, mk=mk: nc.tensor.matmul(
                        bk[:, c * 256:(c + 1) * 256], IDENT, mk,
                        start=False, stop=(c == 1)), tok=(c == 1))
                ex_last = None
                for c in range(2):
                    ex_last = SC(lambda bk=bk, c=c, hgp=hgp: nc.scalar.activation(
                        attnT_sb[hgp][:, c * 256:(c + 1) * 256],
                        bk[:, c * 256:(c + 1) * 256], AF.Exp,
                        bias=0.0, scale=0.125),
                        waits=([mm] + ([attnT_guard[hgp]] if c == 0 else [])))
                bank_close(bk, ex_last)
                avm = None
                for hh in range(2):
                    h = hg * 2 + hh
                    bi = 0 if h < 7 else (1 if h < 14 else 2)
                    co = (h - (0, 7, 14)[bi]) * 68
                    for c in range(2):
                        stf = (h in (0, 7, 14)) and c == 0
                        spf = (h in (6, 13, 15)) and c == 1
                        w = ([ex_last] if (hh == 0 and c == 0) else [])
                        if stf:
                            w = w + av_w[bi]
                        avm = T(lambda bb=av_banks[bi], co=co, c=c, hh=hh,
                                h=h, j=j, hgp=hgp, stf=stf, spf=spf:
                                nc.tensor.matmul(
                            bb[:, co:co + 68],
                            attnT_sb[hgp][:, (c * 2 + hh) * 128:
                                          (c * 2 + hh + 1) * 128],
                            vV65[:, j + c, h, 0:68],
                            start=stf, stop=spf),
                            waits=w, tok=(spf or (hh == 1 and c == 1)))
                        if spf:
                            av_last[bi] = avm
                attnT_guard[hgp] = avm
            for bi, (h0, h1) in enumerate([(0, 7), (7, 14), (14, 16)]):
                src = av_banks[bi][:, 0:(h1 - h0) * 68]
                V(lambda src=src, pt=pt, h0=h0, h1=h1: nc.vector.tensor_copy(
                    mv("sums", pt)[:, h0:h1].unsqueeze(2),
                    src.rearrange("p (h e) -> p h e", e=68)[:, :, 64:65]),
                    waits=[av_last[bi]])
            if ti == 0:
                V(lambda pt=pt: nc.vector.tensor_scalar(
                    mv("sums", pt), mv("sums", pt), mv("row0add")[:, 0:1], None,
                    ALU.add))
            rc = V(lambda pt=pt: nc.vector.reciprocal(mv("recips", pt),
                                                      mv("sums", pt)))
            lg_tok = None
            for bi, (h0, h1) in enumerate([(0, 7), (7, 14), (14, 16)]):
                src = av_banks[bi][:, 0:(h1 - h0) * 68]
                lg_tok = V(lambda src=src, pt=pt, h0=h0, h1=h1:
                           nc.vector.tensor_tensor(
                    mv("lg", pt)[:, h0:h1].unsqueeze(2),
                    src.rearrange("p (h e) -> p h e", e=68)[:, :, 65:66],
                    mv("recips", pt)[:, h0:h1].unsqueeze(2), ALU.mult))
            tistate[(j, 'lg')] = lg_tok
            lv_toks = []
            for bi, (h0, h1) in enumerate([(0, 7), (7, 14), (14, 16)]):
                src = av_banks[bi][:, 0:(h1 - h0) * 68]
                lvt = V(lambda src=src, pt=pt, h0=h0, h1=h1, ti=ti:
                        nc.vector.tensor_tensor(
                    fv_(FO_B + (ti % 2) * 1024 + h0 * 64, (h1 - h0) * 64).rearrange(
                        "p (h e) -> p h e", e=64),
                    src.rearrange("p (h e) -> p h e", e=68)[:, :, 0:64],
                    mv("recips", pt)[:, h0:h1].unsqueeze(2)
                    .broadcast_to([128, h1 - h0, 64]), ALU.mult),
                    waits=[av_last[bi], rc])
                bank_close(av_banks[bi], lvt)
                lv_toks.append(lvt)
            if ti == 0:
                tap("local0", lambda: lv_(ti), [128, 1024], [lv_toks[-1]])
            if ti == 1:
                tap("local1", lambda: lv_(ti), [128, 1024], [lv_toks[-1]])
            if ti == 2:
                tap("local2", lambda ti=ti: lv_(ti), [128, 1024], [lv_toks[-1]])
                tap("sums1", lambda: mv("sums", 1), [128, 16], [rc])

            tistate[(j, 'lv')] = lv_toks
        def stP2(j):
            nonlocal lat_guard
            ti = st * STW + j
            pt = ti % 2
            maskap = maskA0_t if ti == 0 else maskA_t
            if KSUB < 2:
                return
            bk = pbk[2]
            bw = bank_wait(bk) + [rk_tok]
            mm = None
            for lh in range(4):
                r0 = (lh % 2) * 64
                ch = lh // 2
                mm = T(lambda bk=bk, lh=lh, ch=ch, j=j: nc.tensor.matmul(
                    bk[:, lh * 64:(lh + 1) * 64],
                    Rb[:, RO_A + AO_rqT + ch * 256 + j * 128:
                       RO_A + AO_rqT + ch * 256 + (j + 1) * 128].bitcast(F32),
                    Rb[:, RO_rkT + lh * 64:RO_rkT + (lh + 1) * 64].bitcast(F32),
                    start=(lh == 0), stop=(lh == 3)),
                    waits=(bw if lh == 0 else []), tok=(lh == 3))
            vsc = V(lambda bk=bk: nc.vector.tensor_copy(
                mv("latsc"), bk[:, 0:256]), waits=[mm, lat_guard])
            bank_close(bk, vsc)
            vsc = V(lambda pt=pt, ti=ti: nc.vector.tensor_tensor(
                mv("latsc").rearrange("p (l k) -> p l k", l=4),
                mv("latsc").rearrange("p (l k) -> p l k", l=4),
                latmaskB[:, ti * 64:(ti + 1) * 64].unsqueeze(1)
                .broadcast_to([128, 4, 64]), ALU.add))
            V(lambda pt=pt: nc.vector.reduce_max(
                mv("negmaxL", pt), mv("latsc").rearrange("p (l k) -> p l k", l=4),
                axis=AX.X, negate=True))
            vbl = V(lambda pt=pt: nc.vector.tensor_scalar(
                mv("biasL", pt), mv("negmaxL", pt), 0.125, None, ALU.mult))
            exl = None
            for lh in range(4):
                exl = SC(lambda lh=lh, pt=pt: nc.scalar.activation(
                    rv_(RO_A + AO_latexp + lh * 64, 64),
                    mv("latsc")[:, lh * 64:(lh + 1) * 64], AF.Exp,
                    bias=mv("biasL", pt)[:, lh:lh + 1], scale=0.125,
                    accum_out=mv("latsums", pt)[:, lh:lh + 1]),
                    waits=(([vbl, lat_scratch_guard[0]] if lat_scratch_guard[0]
                            else [vbl]) if lh == 0 else []))
            if KLAT < 2:
                return
            lat_guard = exl
            bt = tbk[0]
            btw = bank_wait(bt) + [exl]
            tt = None
            for lh in range(4):
                tt = T(lambda bt=bt, lh=lh: nc.tensor.transpose(
                    bt[0:64, lh * 128:(lh + 1) * 128],
                    rv_(RO_A + AO_latexp + lh * 64, 64), IDENT),
                    waits=(btw if lh == 0 else []), tok=(lh == 3))
            cp = V(lambda bt=bt: nc.vector.tensor_copy(
                Rb[0:64, RO_A + AO_latT:RO_A + AO_latT + 512], bt[0:64, 0:512]),
                waits=[tt, lat_scratch_guard[0]])
            bank_close(bt, cp)
            lr = V(lambda pt=pt: nc.vector.reciprocal(mv("latrecip", pt),
                                                      mv("latsums", pt)))
            if ti == 0:
                lr = V(lambda pt=pt: nc.vector.tensor_scalar(
                    mv("latrecip", pt), mv("latrecip", pt), mv("latrow0")[:, 0:1],
                    None, ALU.mult))
            if KLAT < 3:
                return
            bk = pbk[3]
            bw = bank_wait(bk) + [cp, rvv_tok]
            mm = None
            for lh in range(4):
                mm = T(lambda bk=bk, lh=lh: nc.tensor.matmul(
                    bk[:, lh * 64:(lh + 1) * 64],
                    Rb[0:64, RO_A + AO_latT + lh * 128:
                       RO_A + AO_latT + (lh + 1) * 128].bitcast(F32),
                    Rb[0:64, RO_rv + lh * 64:RO_rv + (lh + 1) * 64].bitcast(F32),
                    start=(lh == 0), stop=(lh == 3)),
                    waits=(bw if lh == 0 else []), tok=(lh == 3))
            vlo = V(lambda bk=bk, pt=pt: nc.vector.tensor_tensor(
                rv_(RO_A + AO_latexp, 256).rearrange("p (l e) -> p l e", l=4),
                bk[:, 0:256].rearrange("p (l e) -> p l e", l=4),
                mv("latrecip", pt).unsqueeze(2).broadcast_to([128, 4, 64]), ALU.mult),
                waits=[mm, lr])
            bank_close(bk, vlo)
            if KLAT < 4:
                return
            bt = tbk[0]
            btw = bank_wait(bt) + [vlo]
            tt = None
            for c in range(2):
                tt = T(lambda bt=bt, c=c: nc.tensor.transpose(
                    bt[:, c * 128:(c + 1) * 128],
                    rv_(RO_A + AO_latexp + c * 128, 128), IDENT),
                    waits=(btw if c == 0 else []), tok=(c == 1))
            cp2 = V(lambda bt=bt: nc.vector.tensor_copy(
                rv_(RO_A + AO_latoutT, 256), bt[:, 0:256]),
                waits=[tt, lat_scratch_guard[0]])
            bank_close(bt, cp2)
            if KLAT < 5:
                return
            bgw = bank_wait(tbk[0]) + [cp2]
            for c in range(2):
                T(lambda c=c: nc.tensor.matmul(
                    tbk[0].bitcast(F32)[:, 0:16],
                    rv_(RO_A + AO_latoutT + c * 128, 128), rwg[:, c * 16:(c + 1) * 16],
                    start=(c == 0), stop=False), waits=(bgw if c == 0 else []))
            mmrg = T(lambda: nc.tensor.matmul(
                tbk[0].bitcast(F32)[:, 0:16], ONES[0:1, 0:128],
                mv("routbg")[0:1, 0:16], start=False, stop=True), tok=True)
            rg_tok = V(lambda pt=pt: nc.vector.tensor_copy(
                mv("rg", pt), tbk[0].bitcast(F32)[:, 0:16]), waits=[mmrg])
            bank_close(tbk[0], rg_tok)
            tistate[(j, 'rg')] = rg_tok
            rem_toks = []
            for half in range(2):
                bk = pbk[2 + half]
                bw = bank_wait(bk) + ([cp2] if half == 0 else [])
                for c in range(2):
                    T(lambda bk=bk, c=c, half=half: nc.tensor.matmul(
                        bk[:, 0:512],
                        rv_(RO_A + AO_latoutT + c * 128, 128),
                        routw[:, c, half * 512:(half + 1) * 512],
                        start=(c == 0), stop=False), waits=(bw if c == 0 else []))
                mm = T(lambda bk=bk, half=half: nc.tensor.matmul(
                    bk[:, 0:512], ONES[0:1, 0:128],
                    mv("routbB")[0:1, half * 512:(half + 1) * 512],
                    start=False, stop=True), tok=True)
                ev = V(lambda bk=bk, half=half: nc.vector.tensor_copy(
                    fv_(FO_B + 2048 + (ti % 2) * 1024 + half * 512, 512), bk[:, 0:512]),
                    waits=[mm, mixed_guard])
                bank_close(bk, ev)
                rem_toks.append(ev)
            if ti == 0:
                tap("remote0", lambda: rv2_(ti), [128, 1024], [rem_toks[-1]])
            if ti == 1:
                tap("remote1", lambda: rv2_(ti), [128, 1024], [rem_toks[-1]])
            if ti == 2:
                tap("remote2", lambda ti=ti: rv2_(ti), [128, 1024], [rem_toks[-1]])

            tistate[(j, 'rem')] = rem_toks
        def stG4(j):
            nonlocal mixed_guard
            ti = st * STW + j
            pt = ti % 2
            maskap = maskA0_t if ti == 0 else maskA_t
            lv_toks = tistate[(j, 'lv')]
            rem_toks = tistate[(j, 'rem')]
            if KSUB < 3:
                return
            vgs = V(lambda pt=pt: nc.vector.tensor_tensor(
                mv("lg", pt), mv("lg", pt), mv("rg", pt), ALU.add),
                waits=[tistate[(j, 'lg')], tistate[(j, 'rg')]])
            sg = SC(lambda pt=pt: nc.scalar.activation(
                mv("gate", pt), mv("lg", pt), AF.Sigmoid, bias=0.0, scale=1.0),
                waits=[vgs])
            V(lambda: nc.vector.tensor_tensor(lv_(ti), lv_(ti), rv2_(ti),
                                              ALU.subtract),
              waits=list(lv_toks) + list(rem_toks))
            V(lambda pt=pt: nc.vector.tensor_tensor(
                lv_(ti).rearrange("p (h e) -> p h e", h=16),
                lv_(ti).rearrange("p (h e) -> p h e", h=16),
                mv("gate", pt).unsqueeze(2).broadcast_to([128, 16, 64]), ALU.mult),
                waits=[sg])
            vmx = V(lambda: nc.vector.tensor_tensor(mixed_v, rv2_(ti), lv_(ti),
                                                    ALU.add),
                    waits=[lnt_guard, mixed_guard])
            if ti == 0:
                tap("mixed0", lambda: mixed_v.bitcast(F32), [128, 1024], [vmx])
            if ti == 1:
                tap("mixed1", lambda: mixed_v.bitcast(F32), [128, 1024], [vmx])
            if ti == 2:
                tap("mixed2", lambda: mixed_v.bitcast(F32), [128, 1024], [vmx])

            if KSUB < 4:
                return
            ow = [vmx]
            if j == 0:
                ow += list(tistate.get((1, 'rem'), []))
                if (1, 'rg') in tistate:
                    ow.append(tistate[(1, 'rg')])
            mtt, mcp = transposes8(
                lambda c: mixed_v[:, c * 128:(c + 1) * 128],
                lambda rnd, j=j: rv_(RO_A + j * 1024, 1024).rearrange(
                    "p (c t) -> p c t", c=8)[:, rnd * 4:(rnd + 1) * 4, :],
                ow)
            mixed_guard = mtt
            tistate[(j, 'mcp')] = mcp

        def stOut():
            banks = [[pbk[0], pbk[1]], [pbk[2], pbk[3]]]
            ao_mm = [[None, None], [None, None]]
            for half in range(2):
                oslots = [ring_load(
                    lambda k=k, half=half:
                    d["out_w"][k * 128:(k + 1) * 128, half * 512:(half + 1) * 512])
                    for k in range(8)]
                bws = [bank_wait(banks[j2][half]) + [tistate[(j2, 'mcp')]]
                       for j2 in range(2)]
                for k in range(8):
                    s_, stk = oslots[k]
                    for j2 in range(2):
                        T(lambda bk=banks[j2][half], s_=s_, k=k, j2=j2:
                          nc.tensor.matmul(
                            bk[:, 0:512],
                            rv_(RO_A + j2 * 1024 + k * 128, 128),
                            slot(s_)[:, 0:512], start=(k == 0), stop=False),
                            waits=(bws[j2] if k == 0 else []) +
                                  ([stk] if j2 == 0 else []))
                mm = None
                for j2 in range(2):
                    mm = T(lambda bk=banks[j2][half], half=half: nc.tensor.matmul(
                        bk[:, 0:512], ONES[0:1, 0:128],
                        Fb[0:1, FO_B + BO_outb + half * 512:
                           FO_B + BO_outb + (half + 1) * 512],
                        start=False, stop=True), tok=True)
                    ao_mm[j2][half] = mm
                ring_release(oslots, mm)
            lat_scratch_guard[0] = ao_mm[1][1]
            for j2 in range(2):
                ti = st * STW + j2
                p, nmtok = ln_stats(lambda j2=j2: banks[j2][0][:, 0:512],
                                    lambda j2=j2: banks[j2][1][:, 0:512],
                                    ao_mm[j2])
                va = V(lambda pt2=p, ti=ti: nc.vector.tensor_scalar(
                    xres[:, ti, :], xres[:, ti, :], mv("nmr", pt2)[:, 0:1], None,
                    ALU.add), waits=[nmtok])
                for half in range(2):
                    va = V(lambda bk=banks[j2][half], p2=p, ti=ti, half=half:
                           nc.vector.scalar_tensor_tensor(
                        xres[:, ti, half * 512:(half + 1) * 512], bk[:, 0:512],
                        mv("rstd", p2)[:, 0:1],
                        xres[:, ti, half * 512:(half + 1) * 512],
                        ALU.mult, ALU.add))
                    bank_close(banks[j2][half], va)
                ln_guard[p] = va
                xres_tok[ti] = va
                if ti == 0:
                    tap("y1_0", lambda: xres[:, 0, :], [128, 1024], [va])

        if st < KATT:
            stP1(0); stP2(0); stP1(1); stP2(1)
            if st < NST - 1:
                emit_LNstats(st + 1)
            stG4(0); stG4(1)
            if KSUB >= 4:
                stOut()
        elif st < NST - 1:
            emit_LNstats(st + 1)
        if st < NST - 1 and st < KATT:
            hg1 = G(lambda: nc.gpsimd.tensor_copy(
                rv_(RO_A + AO_kT, 6144).rearrange(
                    "p (c t) -> p c t", c=16)[:, :, 0:128],
                rv_(RO_A + AO_kT, 6144).rearrange(
                    "p (c t) -> p c t", c=16)[:, :, 256:384]),
                waits=[P.bar("vector"), attnT_guard[0], attnT_guard[1], mixed_guard])
            halo_guard = G(lambda: nc.gpsimd.tensor_copy(
                vVf[:, 0, :], vVf[:, 2, :]), waits=[hg1])

    tap("y1all", lambda: fv_(FO_xres, NT * 1024), [128, NT * 1024],
        [xres_tok[i] for i in range(NT)])

    attn_end = (t_ev, t_ev.n)
    for k in range(RING_X):
        rst["free"][RING_N + k] = attn_end
    rst["active"] = RING_N + RING_X
    GD(lambda: nc.gpsimd.dma_start(
        out=Rb[0:16, RO_A + 12288:RO_A + 12288 + 1024], in_=d["be2"]),
        waits=[attn_end])
    be2_tok = (g_ld, g_ld.n)

    if KPH < 3:
        for ti in range(NT):
            P.emit("gpsimd", (lambda ti=ti: nc.gpsimd.dma_start(
                out=yout[ti * 128:(ti + 1) * 128, :], in_=xres[:, ti, :])),
                waits=[xres_tok[ti]], inc=[so], selfser=False)
        P.emit("gpsimd", lambda: nc.gpsimd.engine_nop(), waits=[(so, 16 * NT)],
               selfser=False)
        for s in P.sems:
            s.h = nc.alloc_semaphore(s.name)
        with nc.Block() as block:
            def mk1(eng):
                def run(proxy):
                    hw = {}
                    for waits, build, incs in P.events[eng]:
                        for sm, v in waits:
                            if hw.get(id(sm), 0) < v:
                                proxy.wait_ge(sm.h, v)
                                hw[id(sm)] = v
                        ins = build()
                        for sm in incs:
                            ins.then_inc(sm.h, sm.step)
                return run
            block.tensor(mk1("tensor"))
            block.vector(mk1("vector"))
            block.scalar(mk1("scalar"))
            block.gpsimd(mk1("gpsimd"))
            block.sync(mk1("sync"))
        return nc, taps

    ln2_guard = None
    yo_guard = None

    for Hf in range(2):
        ln2T_cp = None
        comb_toks = []
        for jj in range(8):
            ti = Hf * 8 + jj
            pt = jj % 2
            p, nmtok = ln_stats(lambda ti=ti: xres[:, ti, 0:512],
                                lambda ti=ti: xres[:, ti, 512:1024],
                                [xres_tok[ti]])
            ap = SC(lambda p=p, ti=ti: nc.scalar.activation(
                ln2_t, xres[:, ti, :], AF.Identity, bias=mv("nmr", p)[:, 0:1],
                scale=mv("rstd", p)[:, 0:1]), waits=[nmtok, ln2_guard])
            ln_guard[p] = ap
            ln2_guard, ln2T_cp = transposes8(
                lambda c: rv_(RO_attnsb + c * 128, 128),
                lambda rnd, jj=jj: ln2T[:, rnd * 4:(rnd + 1) * 4,
                                        jj * 128:(jj + 1) * 128],
                [ap])
            bk = pbk[2]
            bw = bank_wait(bk) + [ln2T_cp, gwT_tok]
            mm = None
            for c in range(8):
                mm = T(lambda bk=bk, c=c, jj=jj: nc.tensor.matmul(
                    bk[:, 0:16],
                    rv_(RO_A + AO_ln2T + c * 1024 + jj * 128, 128).bitcast(F32),
                    rv_(RO_gwT + c * 16, 16).bitcast(F32),
                    start=(c == 0), stop=(c == 7)),
                    waits=(bw if c == 0 else []), tok=(c == 7))
            V(lambda bk=bk, pt=pt: nc.vector.reduce_max(
                mv("scr1a", pt), bk[:, 0:16], axis=AX.X, negate=True), waits=[mm])
            exr = SC(lambda bk=bk, pt=pt: nc.scalar.activation(
                mv("probs", pt), bk[:, 0:16], AF.Exp, bias=mv("scr1a", pt)[:, 0:1],
                scale=1.0, accum_out=mv("scr1b", pt)[:, 0:1]),
                waits=[P.bar("vector")])
            bank_close(bk, exr)
            V(lambda pt=pt: nc.vector.reciprocal(mv("scr1a", pt), mv("scr1b", pt)),
              waits=[exr])
            V(lambda pt=pt: nc.vector.tensor_scalar(
                mv("probs", pt), mv("probs", pt), mv("scr1a", pt)[:, 0:1], None,
                ALU.mult))
            V(lambda pt=pt: nc.vector.tensor_tensor(mv("s2", pt), mv("probs", pt),
                                                    mv("gbB"), ALU.add))
            V(lambda pt=pt: nc.vector.reduce_max(
                mv("scr4a", pt), mv("s2", pt).rearrange("p (g e) -> p g e", g=4),
                axis=AX.X))
            V(lambda pt=pt: nc.vector.tensor_tensor(
                mv("scr16a", pt).rearrange("p (g e) -> p g e", g=4),
                mv("s2", pt).rearrange("p (g e) -> p g e", g=4),
                mv("scr4a", pt).unsqueeze(2).broadcast_to([128, 4, 4]), ALU.is_equal))
            V(lambda pt=pt: nc.vector.scalar_tensor_tensor(
                mv("scr16b", pt), mv("scr16a", pt), NEG, mv("s2", pt),
                ALU.mult, ALU.add))
            V(lambda pt=pt: nc.vector.reduce_max(
                mv("scr4b", pt), mv("scr16b", pt).rearrange("p (g e) -> p g e", g=4),
                axis=AX.X))
            V(lambda pt=pt: nc.vector.tensor_tensor(
                mv("scr4a", pt), mv("scr4a", pt), mv("scr4b", pt), ALU.add))
            V(lambda pt=pt: nc.vector.reduce_max(mv("scr1a", pt), mv("scr4a", pt),
                                                 axis=AX.X))
            V(lambda pt=pt: nc.vector.tensor_tensor(
                mv("scr4b", pt), mv("scr4a", pt),
                mv("scr1a", pt).broadcast_to([128, 4]), ALU.is_equal))
            V(lambda pt=pt: nc.vector.scalar_tensor_tensor(
                mv("scr4c", pt), mv("scr4b", pt), NEG, mv("scr4a", pt),
                ALU.mult, ALU.add))
            V(lambda pt=pt: nc.vector.reduce_max(mv("scr1b", pt), mv("scr4c", pt),
                                                 axis=AX.X))
            V(lambda pt=pt: nc.vector.tensor_tensor(
                mv("scr4c", pt), mv("scr4c", pt),
                mv("scr1b", pt).broadcast_to([128, 4]), ALU.is_equal))
            V(lambda pt=pt: nc.vector.tensor_tensor(
                mv("scr4b", pt), mv("scr4b", pt), mv("scr4c", pt), ALU.add))
            V(lambda pt=pt: nc.vector.scalar_tensor_tensor(
                mv("scr16b", pt).rearrange("p (g e) -> p g e", g=4),
                mv("scr4b", pt).unsqueeze(2).broadcast_to([128, 4, 4]), 1.0,
                mv("zerop").rearrange("p (g e) -> p g e", g=4), ALU.mult, ALU.add))
            V(lambda pt=pt: nc.vector.memset(mv("scr16a", pt), NEG))
            V(lambda pt=pt: nc.vector.copy_predicated(
                mv("scr16a", pt), mv("scr16b", pt).bitcast(mybir.dt.int32),
                mv("s2", pt)))
            V(lambda pt=pt: nc.vector.reduce_max(mv("scr1a", pt), mv("scr16a", pt),
                                                 axis=AX.X))
            V(lambda pt=pt: nc.vector.tensor_tensor(
                mv("scr16b", pt), mv("scr16a", pt),
                mv("scr1a", pt).broadcast_to([128, 16]), ALU.is_equal))
            V(lambda pt=pt: nc.vector.scalar_tensor_tensor(
                mv("scr16a", pt), mv("scr16b", pt), 2 * NEG, mv("scr16a", pt),
                ALU.mult, ALU.add))
            V(lambda pt=pt: nc.vector.reduce_max(mv("scr1b", pt), mv("scr16a", pt),
                                                 axis=AX.X))
            V(lambda pt=pt: nc.vector.tensor_tensor(
                mv("scr16a", pt), mv("scr16a", pt),
                mv("scr1b", pt).broadcast_to([128, 16]), ALU.is_equal))
            V(lambda pt=pt: nc.vector.tensor_tensor(
                mv("scr16b", pt), mv("scr16b", pt), mv("scr16a", pt), ALU.add))
            cmb = V(lambda pt=pt, jj=jj: nc.vector.tensor_tensor(
                Rb[:, RO_A + 14848 + jj * 16:RO_A + 14848 + (jj + 1) * 16],
                mv("scr16b", pt), mv("probs", pt), ALU.mult))
            comb_toks.append(cmb)
            if ti == 0:
                tap("probs0", lambda: mv("probs", 0), [128, 16], [cmb])
                tap("comb0", lambda: Rb[:, RO_A + 14848:RO_A + 14976].bitcast(F32)[:, 0:16], [128, 16], [cmb])
                tap("r_s2", lambda: mv("s2", 0), [128, 16], [cmb])
                tap("r_gsc", lambda: mv("scr4a", 0), [128, 4], [cmb])
                tap("r_keep", lambda: mv("scr4b", 0), [128, 4], [cmb])
                tap("r_osum", lambda: mv("scr16b", 0), [128, 16], [cmb])
            bt = tbk[0]
            btw = bank_wait(bt) + [cmb]
            tt = T(lambda bt=bt, jj=jj: nc.tensor.transpose(
                bt[0:16, 0:128].bitcast(F32), Rb[:, RO_A + 14848:RO_A + 14976].bitcast(F32)[:, jj * 16:(jj + 1) * 16],
                IDENT.bitcast(F32)), waits=btw, tok=True)
            ctv = V(lambda bt=bt: nc.vector.tensor_copy(
                Fb[0:16, misc["combT"][0]:misc["combT"][0] + 128],
                bt[0:16, 0:128].bitcast(F32)), waits=[tt])
            bank_close(bt, ctv)
            for half in range(2):
                bk = pbk[4 + half]
                bw = bank_wait(bk) + [ctv]
                T(lambda bk=bk, half=half: nc.tensor.matmul(
                    bk[:, 0:512], Fb[0:16, misc["combT"][0]:misc["combT"][0] + 128],
                    Rb[0:16, RO_A + 12288 + half * 512:
                       RO_A + 12288 + (half + 1) * 512].bitcast(F32),
                    start=True, stop=False), waits=bw + [be2_tok])
                mm = T(lambda bk=bk, half=half: nc.tensor.matmul(
                    bk[:, 0:512], ONES[0:1, 0:128],
                    BIASP[0:1, 1024 + half * 512:1024 + (half + 1) * 512],
                    start=False, stop=True), tok=True)
                ev = V(lambda bk=bk, jj=jj, half=half: nc.vector.tensor_copy(
                    y_sb[:, jj, half * 512:(half + 1) * 512], bk[:, 0:512]),
                    waits=[mm])
                bank_close(bk, ev)
            if ti == 0:
                tap("yb0", lambda: y_sb[:, 0, :], [128, 1024], [ev])

        tap("ln2T0" if Hf == 0 else "ln2T1",
            lambda: rv_(RO_A + AO_ln2T, 8192).bitcast(F32), [128, 8192], [ln2T_cp])
        tap_blk = []
        for e in range(18):
            hi = e - 16
            if e < 16:
                W1 = lambda k, e=e: d["We1"][e, k * 128:(k + 1) * 128, :]
                W3 = lambda k, e=e: d["We3"][e, k * 128:(k + 1) * 128, :]
                W2 = lambda c, oh, e=e: d["We2"][e, c * 128:(c + 1) * 128,
                                                 oh * 512:(oh + 1) * 512]
                b1 = lambda m, e=e: mv("be1T")[:, m * 16 + e:m * 16 + e + 1]
                b3 = lambda m, e=e: mv("be3T")[:, m * 16 + e:m * 16 + e + 1]
            else:
                W1 = lambda k, hi=hi: d["ws1"][k * 128:(k + 1) * 128,
                                               hi * 512:(hi + 1) * 512]
                W3 = lambda k, hi=hi: d["ws3"][k * 128:(k + 1) * 128,
                                               hi * 512:(hi + 1) * 512]
                W2 = lambda c, oh, hi=hi: d["ws2"][hi * 512 + c * 128:
                                                   hi * 512 + (c + 1) * 128,
                                                   oh * 512:(oh + 1) * 512]
                b1 = lambda m, hi=hi: mv("bs1T")[:, hi * 4 + m:hi * 4 + m + 1]
                b3 = lambda m, hi=hi: mv("bs3T")[:, hi * 4 + m:hi * 4 + m + 1]

            slots1 = [ring_load(lambda k=k, W=W1: W(k)) for k in range(8)]
            h_toks = {}
            for g2 in range(2):
                for m in range(4):
                    bk = pbk[m]
                    bw = bank_wait(bk) + [ln2T_cp]
                    mm = None
                    for k in range(8):
                        s_, stk = slots1[k]
                        mm = T(lambda bk=bk, s_=s_, m=m, k=k, g2=g2:
                               nc.tensor.matmul(
                            bk[:, 0:512], slot(s_)[:, m * 128:(m + 1) * 128],
                            ln2T[:, k, g2 * 512:(g2 + 1) * 512],
                            start=(k == 0), stop=(k == 7)),
                            waits=(bw if k == 0 else []) + [stk], tok=(k == 7))
                    ev = SC(lambda bk=bk, g2=g2, m=m, b1=b1: nc.scalar.activation(
                        hV[:, g2, m, :], bk[:, 0:512], AF.Silu,
                        bias=b1(m), scale=1.0), waits=[mm, yo_guard])
                    bank_close(bk, ev)
                    if g2 == 1 and m == 3:
                        ring_release(slots1, mm)
                    if Hf == 0 and e == 0 and g2 == 0 and m == 0:
                        tap_blk.append(tap("slot0", lambda: rv_(0, 10 * 512)
                                           .bitcast(F32), [128, 5120], [ev]))
                    if Hf == 0 and e == 0 and g2 == 0 and m == 3:
                        tap_blk.append(tap("h1s0", lambda: rv_(RO_A + AO_h, 2048)
                                           .bitcast(F32), [128, 2048], [ev]))
            slots3 = [ring_load(lambda k=k, W=W3: W(k)) for k in range(8)]
            for g2 in range(2):
                for m in range(4):
                    bk = pbk[m]
                    bw = bank_wait(bk) + [t for t in tap_blk if t]

                    mm = None
                    for k in range(8):
                        s_, stk = slots3[k]
                        mm = T(lambda bk=bk, s_=s_, m=m, k=k, g2=g2:
                               nc.tensor.matmul(
                            bk[:, 0:512], slot(s_)[:, m * 128:(m + 1) * 128],
                            ln2T[:, k, g2 * 512:(g2 + 1) * 512],
                            start=(k == 0), stop=(k == 7)),
                            waits=(bw if k == 0 else []) + [stk], tok=(k == 7))
                    ev = V(lambda bk=bk, g2=g2, m=m, b3=b3:
                           nc.vector.scalar_tensor_tensor(
                        hV[:, g2, m, :], bk[:, 0:512], b3(m), hV[:, g2, m, :],
                        ALU.add, ALU.mult), waits=[mm, P.bar("scalar")])
                    h_toks[(g2, m)] = ev
                    bank_close(bk, ev)
                    if g2 == 1 and m == 3:
                        ring_release(slots3, mm)
            slots2 = {}
            for c in range(4):
                for oh in range(2):
                    slots2[(c, oh)] = ring_load(lambda c=c, oh=oh, W=W2: W(c, oh))
            for g2 in range(2):
                for j2 in range(4):
                    jj2 = g2 * 4 + j2
                    for oh in range(2):
                        bk = pbk[4 + ((j2 * 2 + oh) % 2)]
                        bw = bank_wait(bk) + [h_toks[(g2, 3)]]
                        mm = None
                        for c in range(4):
                            s_, stk = slots2[(c, oh)]
                            mm = T(lambda bk=bk, s_=s_, c=c, g2=g2, j2=j2:
                                   nc.tensor.matmul(
                                bk[:, 0:512],
                                hV[:, g2, c, j2 * 128:(j2 + 1) * 128],
                                slot(s_)[:, 0:512],
                                start=(c == 0), stop=(c == 3)),
                                waits=(bw if c == 0 else []) + [stk], tok=(c == 3))
                        yo_guard = mm
                        if e < 16:
                            ev = V(lambda bk=bk, jj2=jj2, oh=oh, e=e:
                                   nc.vector.scalar_tensor_tensor(
                                y_sb[:, jj2, oh * 512:(oh + 1) * 512], bk[:, 0:512],
                                Rb[:, RO_A + 14848:RO_A + 14976].bitcast(F32)[:, jj2 * 16 + e:jj2 * 16 + e + 1],
                                y_sb[:, jj2, oh * 512:(oh + 1) * 512],
                                ALU.mult, ALU.add),
                                waits=[mm] + comb_toks[jj2:jj2 + 1])
                        else:
                            ev = V(lambda bk=bk, jj2=jj2, oh=oh:
                                   nc.vector.tensor_tensor(
                                y_sb[:, jj2, oh * 512:(oh + 1) * 512],
                                y_sb[:, jj2, oh * 512:(oh + 1) * 512],
                                bk[:, 0:512], ALU.add), waits=[mm])
                        bank_close(bk, ev)
            ring_release(list(slots2.values()), yo_guard)
            if Hf == 0 and e == 0 and "h0" in TAP_NAMES:
                tap("h0", lambda: rv_(RO_A + AO_h, 2048).bitcast(F32), [128, 2048],
                    [h_toks[(0, 3)]])

        for jj in range(8):
            ti = Hf * 8 + jj
            p, nmtok = ln_stats(lambda jj=jj: y_sb[:, jj, 0:512],
                                lambda jj=jj: y_sb[:, jj, 512:1024],
                                [P.bar("vector")])
            va = V(lambda p=p, ti=ti: nc.vector.tensor_scalar(
                xres[:, ti, :], xres[:, ti, :], mv("nmr", p)[:, 0:1], None, ALU.add),
                waits=[nmtok])
            va = V(lambda p=p, jj=jj, ti=ti: nc.vector.scalar_tensor_tensor(
                xres[:, ti, :], y_sb[:, jj, :], mv("rstd", p)[:, 0:1],
                xres[:, ti, :], ALU.mult, ALU.add))
            ln_guard[p] = va
            P.emit("gpsimd", (lambda ti=ti: nc.gpsimd.dma_start(
                out=yout[ti * 128:(ti + 1) * 128, :], in_=xres[:, ti, :])),
                waits=[va], inc=[so], selfser=False)
            if ti == 0:
                tap("y0", lambda: y_sb[:, 0, :], [128, 1024], [va])

    P.emit("gpsimd", lambda: nc.gpsimd.engine_nop(), waits=[(so, 16 * NT)],
           selfser=False)

    for s in P.sems:
        s.h = nc.alloc_semaphore(s.name)

    with nc.Block() as block:
        def make(eng):
            def run(proxy):
                hw = {}
                for waits, build, incs in P.events[eng]:
                    for sm, v in waits:
                        if hw.get(id(sm), 0) < v:
                            proxy.wait_ge(sm.h, v)
                            hw[id(sm)] = v
                    ins = build()
                    for sm in incs:
                        ins.then_inc(sm.h, sm.step)
            return run
        block.tensor(make("tensor"))
        block.vector(make("vector"))
        block.scalar(make("scalar"))
        block.gpsimd(make("gpsimd"))
        block.sync(make("sync"))

    return nc, taps


_CACHE = {}


def _host_inputs(core, inp):
    b, q = core // 4, core % 4
    x = np.asarray(inp["L0_attn_x"])
    g0 = q * R
    m = {}
    m["xfull"] = np.ascontiguousarray(x[b])
    m["xown"] = np.ascontiguousarray(x[b, g0:g0 + R])
    m["xhalo"] = np.ascontiguousarray(x[b, g0 - 128:g0] if q > 0 else x[b, 0:128])
    for nm, key in [("qkv_w", "L0_attn_qkv_w"), ("rq_w", "L0_attn_rq_w"),
                    ("rkv_w", "L0_attn_rkv_w"), ("rout_w", "L0_attn_rout_w"),
                    ("out_w", "L0_attn_out_w"), ("gate_w", "L0_ffn_gate_w"),
                    ("We1", "L0_ffn_We1"), ("We2", "L0_ffn_We2"),
                    ("We3", "L0_ffn_We3"), ("ws1", "L0_ffn_ws1"),
                    ("ws2", "L0_ffn_ws2"), ("ws3", "L0_ffn_ws3")]:
        m[nm] = np.ascontiguousarray(np.asarray(inp[key], dtype=np.float32))
    qkv_b = np.asarray(inp["L0_attn_qkv_b"], dtype=np.float32)
    m["qkbT"] = np.ascontiguousarray(qkv_b[0:2048].reshape(16, 128).T)
    m["rqbT"] = np.ascontiguousarray(
        np.asarray(inp["L0_attn_rq_b"], np.float32).reshape(2, 128).T)
    m["rkvbT"] = np.ascontiguousarray(
        np.asarray(inp["L0_attn_rkv_b"], np.float32).reshape(4, 128).T)
    be1 = np.asarray(inp["L0_ffn_be1"], np.float32)
    be3 = np.asarray(inp["L0_ffn_be3"], np.float32)
    m["be1T"] = np.ascontiguousarray(be1.reshape(16, 4, 128).transpose(2, 1, 0)
                                     .reshape(128, 64))
    m["be3T"] = np.ascontiguousarray(be3.reshape(16, 4, 128).transpose(2, 1, 0)
                                     .reshape(128, 64))
    bs1 = np.asarray(inp["L0_ffn_bs1"], np.float32)
    bs3 = np.asarray(inp["L0_ffn_bs3"], np.float32)
    m["bs1T"] = np.ascontiguousarray(bs1.reshape(8, 128).T)
    m["bs3T"] = np.ascontiguousarray(bs3.reshape(8, 128).T)
    m["be2"] = np.ascontiguousarray(np.asarray(inp["L0_ffn_be2"], np.float32))
    m["gbB"] = np.ascontiguousarray(
        np.tile(np.asarray(inp["L0_ffn_gate_b"], np.float32)[None, :], (128, 1)))
    gl = np.asarray(inp["L0_attn_gl_w"], np.float32)[:, 0]
    gr = np.asarray(inp["L0_attn_gr_w"], np.float32)[:, 0]
    m["glwB"] = np.ascontiguousarray(np.tile(np.tile(gl, 16)[None, :], (128, 1)))
    m["grwB"] = np.ascontiguousarray(np.tile(np.tile(gr, 16)[None, :], (128, 1)))
    lat = np.zeros((NT, 64), np.float32)
    for i in range(NT):
        lat[i, q * 16 + i:] = NEG
    m["latmaskB"] = np.ascontiguousarray(np.tile(lat.reshape(1, -1), (128, 1)))
    m["identr"] = np.eye(128, dtype=np.float32)
    m["oneover"] = np.full((128, 1), 1.0 / 128, np.float32)
    m["epsc"] = np.full((128, 1), EPS, np.float32)
    ones = np.zeros((128, 128), np.float32)
    ones[[0, 32, 64], :] = 1.0
    m["onespage"] = ones
    bp = np.zeros((128, 2048), np.float32)
    bp[0, 0:1024] = qkv_b[2048:3072]
    bp[0, 1024:2048] = np.asarray(inp["L0_ffn_bs2"], np.float32)
    m["biaspage"] = bp
    ob = np.zeros((128, 1024), np.float32)
    ob[0, :] = np.asarray(inp["L0_attn_out_b"], np.float32)
    m["outbB"] = ob
    rb2 = np.zeros((128, 1024), np.float32)
    rb2[0, :] = np.asarray(inp["L0_attn_rout_b"], np.float32)
    m["routbB"] = rb2
    mA = np.zeros((128, 256), np.float32)
    qq = np.arange(128)[:, None]
    kk = np.arange(128)[None, :]
    mA[:, 0:128] = np.where(kk >= qq, 0.0, NEG)
    mA[:, 128:256] = np.where(kk < qq, 0.0, NEG)
    mA0 = mA.copy()
    if q == 0:
        mA0[:, 0:128] = NEG

    def maskT(msk):
        b0 = np.ascontiguousarray(msk[:, 0:128].T)
        b1 = np.ascontiguousarray(msk[:, 128:256].T)
        return np.concatenate([b0, b0, b1, b1], axis=1).astype(np.float32)

    m["maskA"] = maskT(mA)
    m["maskA0"] = maskT(mA0)[:, 0:256]
    rm = np.ones((128, 1), np.float32)
    ra = np.zeros((128, 1), np.float32)
    if q == 0:
        rm[0, 0] = 0.0
        ra[0, 0] = 1.0
    m["rowmask0"] = rm
    m["row0add"] = ra
    m["latrow0"] = (np.zeros if q == 0 else np.ones)((128, 1)).astype(np.float32)
    return m


def kernel(**inputs):
    key = "nc"
    if key not in _CACHE:
        _CACHE[key] = build_program()
    nc, taps = _CACHE[key]
    in_maps = [_host_inputs(c, inputs) for c in range(8)]
    res = run_bass_kernel_spmd(nc, in_maps, core_ids=list(range(8)))
    out = np.empty((B, S, D), np.float32)
    for c in range(8):
        b, q = c // 4, c % 4
        out[b, q * R:(q + 1) * R] = res.results[c]["yout"]
    if taps:
        kernel.tap_results = [{t: res.results[c].get("tap_" + t) for t in taps}
                              for c in range(8)]
    return out


if __name__ == "__main__":
    import importlib.util
    spec = importlib.util.spec_from_file_location("_ref", "/root/problem/reference.py")
    ref = importlib.util.module_from_spec(spec)
    spec.loader.exec_module(ref)
    ins = {k: np.asarray(v) for k, v in ref.setup_inputs().items()}
    got = kernel(**ins)
    import jax.numpy as jnp
    want = np.asarray(ref.reference(**{k: jnp.asarray(v) for k, v in ins.items()}))
    err = np.linalg.norm(got - want) / np.linalg.norm(want)
    print("rel_err", err)
